# version 19
# baseline (speedup 1.0000x reference)
import contextlib
import math
import numpy as np
import ml_dtypes

import concourse.bass as bass
import concourse.mybir as mybir
from concourse.ap import AP
from concourse.bass_utils import run_bass_kernel_spmd

F32 = mybir.dt.float32
F16 = mybir.dt.float16
BF16 = mybir.dt.bfloat16
I32 = mybir.dt.int32
U8 = mybir.dt.uint8
U16 = mybir.dt.uint16
AF = mybir.ActivationFunctionType
ALU = mybir.AluOpType

BF = ml_dtypes.bfloat16

B, S, D = 4, 4096, 2048
NCORES = 8
TPC = 2048
NB = 4
H, KVH, HD = 16, 4, 128
GRP = H // KVH
WIN = 512
I_FF = 5632
LN_EPS = 1e-5
RMS_EPS_FFN = 1e-6
RMS_EPS_QK = 1.1920928955078125e-07
SHIFT_B = 40.0
ROPE_A_INV = (10000.0 ** (-(np.arange(64, dtype=np.float64) / 64.0)))
ROPE_B_INV = (10000.0 ** (-(np.arange(0, 2048, 2, dtype=np.float64)[:64]
                            / 2048.0)))

DEBUG_TAPS = ()
PHASE_LIMIT = None
QKV_FLAGS = dict(nblk=4, do_q=True, do_kv=True, do_rope=True)


class Sem:
    def __init__(self, handle):
        self.h = handle
        self.n = 0

    def bump(self, k):
        self.n += k
        return self.n


class Script:
    ENGINES = ("sync", "tensor", "vector", "scalar", "gpsimd")

    def __init__(self):
        self.items = {e: [] for e in self.ENGINES}

    def on(self, engine, fn, inc=None, inc_k=1):
        if inc is None:
            self.items[engine].append(fn)
            return None
        n = inc.bump(inc_k)
        self.items[engine].append(
            lambda eng, f=fn, s=inc.h, k=inc_k: f(eng).then_inc(s, k))
        return n

    def chain(self, engine, sem):
        n = self.inc_last(engine, sem)
        self.wait(engine, sem, n)

    def inc_last(self, engine, sem, k=1):
        n = sem.bump(k)
        prev = self.items[engine][-1]
        self.items[engine][-1] = (
            lambda eng, f=prev, s=sem.h, kk=k: f(eng).then_inc(s, kk))
        return n

    def wait(self, engine, sem, val=None):
        v = sem.n if val is None else val
        if v > 0:
            self.items[engine].append(
                lambda eng, s=sem.h, v=v: eng.wait_ge(s, v))

    def run(self, block):
        for name in self.ENGINES:
            fns = self.items[name]
            getattr(block, name)(lambda eng, fns=fns: [f(eng) for f in fns])


class _StopPlanning(Exception):
    pass


def build_nc(taps=(), phase_limit=None):
    nc = bass.Bass("TRN2", target_bir_lowering=False,
                   detect_race_conditions=False)
    _uid = [0]

    def U(s):
        _uid[0] += 1
        return f"{s}_{_uid[0]}"

    def inp(name, shape, dt):
        return nc.dram_tensor(name, shape, dt, kind="ExternalInput").ap()

    def intl(name, shape, dt, shared=False):
        if shared:
            return nc.dram_tensor(name, shape, dt, kind="Internal",
                                  addr_space="Shared").ap()
        return nc.dram_tensor(name, shape, dt, kind="Internal").ap()

    x_own = inp("x_own", [TPC, D], F32)
    x_halo = inp("x_halo", [NB * 512, D], F32)
    wq_a_t = inp("wq_a_t", [D, D], F32)
    wk_a_t = inp("wk_a_t", [D, 512], F32)
    wv_a_t = inp("wv_a_t", [D, 512], F32)
    wo_a_t = inp("wo_a_t", [D, D], F32)
    qgain = inp("qgain", [1, 16], F32)
    wq_b = inp("wq_b", [D, D], F32)
    bq_b = inp("bq_b", [1, D], F32)
    wk_b = inp("wk_b", [D, 512], F32)
    bk_b = inp("bk_b", [1, 512], F32)
    wv_b = inp("wv_b", [D, 512], F32)
    bv_b = inp("bv_b", [1, 512], F32)
    wo_b = inp("wo_b", [D, D], F32)
    bo_b = inp("bo_b", [1, D], F32)
    gup_pk = inp("gup_pk", [2 * I_FF // 8, 256], I32)
    gup_sct = inp("gup_sct", [2 * I_FF // 8, 16], F16)
    dn_pk = inp("dn_pk", [D // 8, 704], I32)
    dn_sct = inp("dn_sct", [D // 8, 44], F16)
    cos_aq = inp("cos_aq", [TPC, 64], F32)
    sin_aq = inp("sin_aq", [TPC, 64], F32)
    cos_ak = inp("cos_ak", [NB * 1024, 64], F32)
    sin_ak = inp("sin_ak", [NB * 1024, 64], F32)
    cos_b = inp("cos_b", [TPC, 64], F32)
    sin_b = inp("sin_b", [TPC, 64], F32)
    mask_a = inp("mask_a", [NB, 8, 128, 512], BF16)
    mask_b = inp("mask_b", [NB, 8, 128, 512], BF16)

    out = nc.dram_tensor("out", [TPC, D], F32, kind="ExternalOutput").ap()

    x1 = intl("x1", [TPC, D], F32)
    x2 = intl("x2", [TPC, D], F32)
    x3 = intl("x3", [TPC, D], F32)
    xln = intl("xln", [TPC, D], BF16)
    xln_a = intl("xln_a", [NB * 1024, D], BF16)
    qa = intl("qa", [TPC, D], BF16)
    ka_buf = intl("ka_buf", [NB * 1024, 512], BF16)
    va_buf = intl("va_buf", [NB * 1024, 512], BF16)
    qb = intl("qb", [TPC, D], BF16)
    kb_own = intl("kb_own", [TPC, 512], BF16)
    vb_own = intl("vb_own", [TPC, 512], BF16)
    ag_k = intl("ag_k", [2 * TPC, 512], BF16)
    ag_v = intl("ag_v", [2 * TPC, 512], BF16)
    w_gu = intl("w_gu", [2 * I_FF, D], BF16)
    w_dn = intl("w_dn", [D, I_FF], BF16)
    w_gu_sh = intl("w_gu_sh", [2 * I_FF // 8, D], BF16)
    w_dn_sh = intl("w_dn_sh", [D // 8, I_FF], BF16)
    y_scr = intl("y_scr", [1024, D], F32)
    wbf = {}
    for name, src_ap, shape in (
            ("wq_a_t", wq_a_t, [D, D]), ("wk_a_t", wk_a_t, [D, 512]),
            ("wv_a_t", wv_a_t, [D, 512]), ("wo_a_t", wo_a_t, [D, D]),
            ("wq_b", wq_b, [D, D]), ("wk_b", wk_b, [D, 512]),
            ("wv_b", wv_b, [D, 512]), ("wo_b", wo_b, [D, D])):
        wbf[name] = (src_ap, src_ap)

    tap_map = {}
    if taps:
        tapable = dict(x1=x1, x2=x2, x3=x3, xln_a=xln_a, xln=xln, qa=qa,
                       ka=ka_buf, va=va_buf, qb=qb, kb=kb_own, vb=vb_own,
                       agk=ag_k, agv=ag_v, wgu=w_gu, wdn=w_dn)
        for t in taps:
            src = tapable[t]
            tap_map[t] = (src, nc.dram_tensor(
                "tap_" + t, list(src.shape), src.dtype,
                kind="ExternalOutput").ap())

    ones_bf_d = nc.inline_tensor(np.ones((128, 128), dtype=BF),
                                 name="ones_bf").ap()
    ones_f32_d = nc.inline_tensor(np.ones((1, 128), np.float32),
                                  name="ones_f32").ap()

    with contextlib.ExitStack() as top:
        ec = top.enter_context
        sb_ones = ec(nc.sbuf_tensor(U("sb_ones"), [128, 128], BF16))
        sb_ones_f32 = ec(nc.sbuf_tensor(U("sb_ones_f32"), [1, 128], F32))
        sb_c7 = ec(nc.sbuf_tensor(U("sb_c7"), [128, 1], I32))
        sb_c8 = ec(nc.sbuf_tensor(U("sb_c8"), [128, 1], I32))
        sb_gsc = ec(nc.sbuf_tensor(U("sb_gsc"), [128, 16], F32))
        sb_gbias = ec(nc.sbuf_tensor(U("sb_gbias"), [128, 16], F32))
        CONSTS = [LN_EPS, 1.0 / 128.0, RMS_EPS_QK, 1.0 / D, RMS_EPS_FFN,
                  -SHIFT_B]
        sb_const = ec(nc.sbuf_tensor(U("sb_const"), [128, len(CONSTS)], F32))
        for ci, cv in enumerate(CONSTS):
            nc.const_aps.register(F32, cv, sb_const[:, ci:ci + 1])

        sems = {k: Sem(ec(nc.semaphore(k)))
                for k in ("ld", "st", "ldf", "stf", "mm", "act", "vec",
                          "cc", "ccf", "col", "pl", "ph", "cw")}
        ld, st, ldf, stf, mm_s, act, vec, cc, ccf, col, pl, ph, cw = (
            sems[k] for k in ("ld", "st", "ldf", "stf", "mm", "act", "vec",
                              "cc", "ccf", "col", "pl", "ph", "cw"))

        def lflag(sc):
            sc.wait("sync", ld, ld.n)
            return sc.on("sync", lambda eng: eng.nop(), inc=ldf)

        def sflag(sc):
            sc.wait("sync", st, st.n)
            return sc.on("sync", lambda eng: eng.nop(), inc=stf)

        def cflag(sc):
            sc.wait("gpsimd", cc, cc.n)
            return sc.on("gpsimd", lambda eng: eng.nop(), inc=ccf)

        def fence(sc):
            fl = lflag(sc)
            fs = sflag(sc)
            sc.wait("gpsimd", cc, cc.n)
            gp = sc.on("gpsimd", lambda eng: eng.nop(), inc=ccf)
            for e in Script.ENGINES:
                sc.wait(e, ldf, fl)
                sc.wait(e, stf, fs)
                sc.wait(e, ccf, gp)
                for s in (mm_s, act, vec, pl):
                    sc.wait(e, s)

        def barrier(sc):
            fence(sc)
            for e in Script.ENGINES:
                ph.bump(1)
                sc.items[e].append(
                    lambda eng, s=ph.h: eng.nop().then_inc(s, 1))
            target = ph.n

            def start_waits(next_sc):
                for e in Script.ENGINES:
                    next_sc.items[e].append(
                        lambda eng, s=ph.h, v=target: eng.wait_ge(s, v))
            return start_waits

        def dma(sc, engine, sem, dst, src, transpose=False):
            def f(eng, d=dst, s=src, t=transpose):
                return (eng.dma_start(d, s, transpose=True) if t
                        else eng.dma_start(d, s))
            return sc.on(engine, f, inc=sem, inc_k=16)

        phases = []

        def check_done():
            if phase_limit is not None and len(phases) >= phase_limit:
                raise _StopPlanning()

        sc = Script()
        dma(sc, "sync", ld, sb_ones[:, :], ones_bf_d[:, :])
        n0 = dma(sc, "sync", ld, sb_ones_f32[:, :], ones_f32_d[:, :])
        sc.on("vector", lambda eng: eng.memset(sb_c7[:, :], 7))
        sc.on("vector", lambda eng: eng.memset(sb_c8[:, :], 8), inc=vec)
        for ci, cv in enumerate(CONSTS):
            sc.on("vector", lambda eng, ci=ci, cv=cv: eng.memset(
                sb_const[:, ci:ci + 1], cv))
        sc.on("vector", lambda eng: eng.nop(), inc=vec)


        with contextlib.ExitStack() as ps0:
            pec = ps0.enter_context
            sb_g = pec(nc.sbuf_tensor(U("p0_g"), [1, 16], F32))
            sb_gt = pec(nc.sbuf_tensor(U("p0_gt"), [1, 16], F32))
            ps_g = pec(nc.psum_tensor(U("p0_psg"), [128, 16], F32))
            dma(sc, "sync", ld, sb_g[:, :], qgain[:, :])
            fg = lflag(sc)
            sc.wait("scalar", ldf, fg)
            na = sc.on("scalar", lambda eng: eng.activation(
                sb_gt[:, :], sb_g[:, :], AF.Abs), inc=act)
            sc.wait("vector", act, na)
            sc.on("vector", lambda eng: eng.tensor_scalar(
                sb_gt[:, :], sb_gt[:, :], math.sqrt(128.0), 40.0,
                ALU.mult, ALU.min))
            sc.chain("vector", vec)
            sc.on("vector", lambda eng: eng.tensor_scalar(
                sb_gt[:, :], sb_gt[:, :], -1.0, None, ALU.mult))
            sc.chain("vector", vec)
            nv = sc.on("vector", lambda eng: eng.tensor_scalar(
                sb_g[:, :], sb_g[:, :], 1.0 / math.sqrt(128.0), None,
                ALU.mult), inc=vec)
            sc.wait("tensor", vec, nv)
            sc.wait("tensor", ldf, fg)
            nm = sc.on("tensor", lambda eng: eng.matmul(
                ps_g[:, :], sb_ones_f32[:, :], sb_gt[:, :],
                start=True, stop=True), inc=mm_s)
            sc.wait("vector", mm_s, nm)
            nv = sc.on("vector", lambda eng: eng.tensor_copy(
                sb_gbias[:, :], ps_g[:, :]), inc=vec)
            sc.wait("tensor", vec, nv)
            nm = sc.on("tensor", lambda eng: eng.matmul(
                ps_g[:, :], sb_ones_f32[:, :], sb_g[:, :],
                start=True, stop=True), inc=mm_s)
            sc.wait("vector", mm_s, nm)
            sc.on("vector", lambda eng: eng.tensor_copy(
                sb_gsc[:, :], ps_g[:, :]), inc=vec)

            def plan_dequant(pk_ap, sct_ap, w_ap, nrows, words):
                nib = 4 * words
                groups = (8 * words) // 128
                with contextlib.ExitStack() as ds:
                    dec = ds.enter_context
                    t_pk = dec(nc.sbuf_tensor(U("dq_pk"), [128, words], I32))
                    t_lo = dec(nc.sbuf_tensor(U("dq_lo"), [128, words], I32))
                    t_hi = dec(nc.sbuf_tensor(U("dq_hi"), [128, words], I32))
                    t_e = dec(nc.sbuf_tensor(U("dq_e"), [128, nib], U16))
                    t_n7 = dec(nc.sbuf_tensor(U("dq_n7"), [128, nib], U16))
                    t_t1 = dec(nc.sbuf_tensor(U("dq_t1"), [128, nib], U16))
                    t_t3 = dec(nc.sbuf_tensor(U("dq_t3"), [128, nib], U16))
                    t_sg = dec(nc.sbuf_tensor(U("dq_sg"), [128, nib], U16))
                    t_lh = dec(nc.sbuf_tensor(U("dq_lh"), [128, 2 * nib], U16))
                    t_w = dec(nc.sbuf_tensor(U("dq_w"), [128, 2 * nib], U16))
                    t_s16 = dec(nc.sbuf_tensor(U("dq_s16"), [128, groups], F16))
                    t_sc = dec(nc.sbuf_tensor(U("dq_sc"), [128, groups], F32))
                    t_wo = dec(nc.sbuf_tensor(U("dq_wo"), [128, 2 * nib], BF16))
                    il_src = AP(t_lh[:, :].tensor, t_lh[:, :].offset,
                                [list(t_lh[:, :].ap[0]), [1, nib], [nib, 2]])
                    prev_store = [0]
                    for r in range(nrows // 128):
                        r0 = r * 128
                        sc.wait("vector", stf, prev_store[0])
                        dma(sc, "sync", ld, t_pk[:, :],
                            pk_ap[r0:r0 + 128, :])
                        dma(sc, "sync", ld, t_s16[:, :],
                            sct_ap[r0:r0 + 128, :])
                        fl = lflag(sc)
                        sc.wait("vector", ldf, fl)
                        sc.on("vector", lambda eng: eng.tensor_copy(
                            t_sc[:, :], t_s16[:, :]))
                        sc.on("vector", lambda eng: eng.tensor_scalar(
                            t_lo[:, :], t_pk[:, :], 0x0F0F0F0F, None,
                            ALU.bitwise_and))
                        sc.on("vector", lambda eng: eng.tensor_scalar(
                            t_hi[:, :], t_pk[:, :], 4, 0x0F0F0F0F,
                            ALU.logical_shift_right, ALU.bitwise_and))
                        for plane, tsrc in ((0, t_lo), (1, t_hi)):
                            lhs = t_lh[:, plane * nib:(plane + 1) * nib]
                            sc.on("vector", lambda eng, s=tsrc:
                                  eng.tensor_copy(t_e[:, :],
                                                  s[:, :].bitcast(U8)))
                            sc.on("vector", lambda eng: eng.tensor_scalar(
                                t_n7[:, :], t_e[:, :], sb_c7[:, :], None,
                                ALU.bitwise_and))
                            sc.on("vector", lambda eng: eng.tensor_scalar(
                                t_t1[:, :], t_n7[:, :], 64, 16128,
                                ALU.mult, ALU.add))
                            sc.on("vector", lambda eng: eng.tensor_scalar(
                                t_t3[:, :], t_n7[:, :], 4, 16128,
                                ALU.min, ALU.mult))
                            sc.on("vector", lambda eng: eng.tensor_tensor(
                                t_t1[:, :], t_t1[:, :], t_t3[:, :], ALU.min))
                            sc.on("vector", lambda eng: eng.tensor_scalar(
                                t_sg[:, :], t_e[:, :], sb_c8[:, :], None,
                                ALU.bitwise_and))
                            sc.on("vector", lambda eng: eng.tensor_scalar(
                                t_sg[:, :], t_sg[:, :], 4096, None, ALU.mult))
                            sc.on("vector", lambda eng, o=lhs:
                                  eng.tensor_tensor(o, t_t1[:, :], t_sg[:, :],
                                                    ALU.add))
                        sc.on("vector", lambda eng: eng.tensor_copy(
                            t_w[:, :], il_src))
                        wv_ = t_w[:, :].bitcast(BF16)
                        for g in range(groups):
                            last = sc.on(
                                "vector",
                                lambda eng, g=g: eng.tensor_scalar(
                                    t_wo[:, g * 128:(g + 1) * 128],
                                    wv_[:, g * 128:(g + 1) * 128],
                                    t_sc[:, g:g + 1], None, ALU.mult),
                                inc=(vec if g == groups - 1 else None))
                        sc.wait("sync", vec, last)
                        dma(sc, "sync", st, w_ap[r0:r0 + 128, :], t_wo[:, :])
                        prev_store[0] = sflag(sc)

            plan_dequant(gup_pk, gup_sct, w_gu_sh, 2 * I_FF // 8, 256)
            plan_dequant(dn_pk, dn_sct, w_dn_sh, D // 8, 704)

            start_next = barrier(sc)
            phases.append(sc)
            check_done()

        def plan_ln(sc, srcs, dst_ap, n_tiles):
            with contextlib.ExitStack() as ls:
                lec = ls.enter_context
                t_x = [lec(nc.sbuf_tensor(U(f"ln_x{i}"), [128, D], F32))
                       for i in range(2)]
                t_o = [lec(nc.sbuf_tensor(U(f"ln_o{i}"), [128, D], BF16))
                       for i in range(2)]
                t_sq = lec(nc.sbuf_tensor(U("ln_sq"), [128, D], BF16))
                t_ss = lec(nc.sbuf_tensor(U("ln_ss"), [128, 1], F32))
                t_s = lec(nc.sbuf_tensor(U("ln_s"), [128, 1], F32))
                t_m = lec(nc.sbuf_tensor(U("ln_m"), [128, 1], F32))
                t_v = lec(nc.sbuf_tensor(U("ln_v"), [128, 1], F32))
                t_sd = lec(nc.sbuf_tensor(U("ln_sd"), [128, 1], F32))
                t_r = lec(nc.sbuf_tensor(U("ln_r"), [128, 1], F32))
                hist = [0, 0]
                for i in range(n_tiles):
                    xt = t_x[i % 2]
                    ot = t_o[i % 2]
                    sc.wait("sync", vec, hist[i % 2])
                    dma(sc, "sync", ld, xt[:, :], srcs(i))
                    fl = lflag(sc)
                    sc.wait("scalar", vec, vec.n)
                    sc.wait("scalar", ldf, fl)
                    na = sc.on("scalar", lambda eng, x=xt: eng.activation(
                        t_sq[:, :], x[:, :], AF.Square,
                        accum_out=t_ss[:, :]), inc=act)
                    sc.wait("vector", ldf, fl)
                    sc.on("vector", lambda eng, x=xt: eng.reduce_sum(
                        t_s[:, :], x[:, :], axis=mybir.AxisListType.X))
                    sc.chain("vector", vec)
                    sc.wait("vector", act, na)
                    sc.on("vector", lambda eng: eng.tensor_scalar(
                        t_m[:, :], t_s[:, :], 1.0 / D, None, ALU.mult))
                    sc.chain("vector", vec)
                    sc.on("vector", lambda eng: eng.tensor_tensor(
                        t_v[:, :], t_m[:, :], t_m[:, :], ALU.mult))
                    sc.chain("vector", vec)
                    sc.on("vector", lambda eng: eng.tensor_scalar(
                        t_s[:, :], t_ss[:, :], 1.0 / D, None, ALU.mult))
                    sc.chain("vector", vec)
                    nv = sc.on("vector", lambda eng: eng.tensor_tensor(
                        t_v[:, :], t_s[:, :], t_v[:, :], ALU.subtract),
                        inc=vec)
                    sc.wait("scalar", vec, nv)
                    na2 = sc.on("scalar", lambda eng: eng.activation(
                        t_sd[:, :], t_v[:, :], AF.Sqrt, bias=LN_EPS), inc=act)
                    sc.wait("vector", act, na2)
                    sc.on("vector", lambda eng: eng.reciprocal(
                        t_r[:, :], t_sd[:, :]))
                    sc.chain("vector", vec)
                    nv2 = sc.on("vector", lambda eng, x=xt, o=ot:
                                eng.tensor_scalar(
                                    o[:, :], x[:, :], t_m[:, :], t_r[:, :],
                                    ALU.subtract, ALU.mult), inc=vec)
                    hist[i % 2] = nv2
                    sc.wait("sync", vec, nv2)
                    dma(sc, "sync", st, dst_ap[i * 128:(i + 1) * 128, :],
                        ot[:, :])
                sc.wait("sync", st, st.n)

        try:
            sc = Script()
            start_next(sc)

            def ln_a_src(i):
                blk, sub = divmod(i, 8)
                if sub < 4:
                    r = blk * 512 + sub * 128
                    return x_halo[r:r + 128, :]
                r = blk * 512 + (sub - 4) * 128
                return x_own[r:r + 128, :]
            plan_ln(sc, ln_a_src, xln_a, NB * 8)
            start_next = barrier(sc)
            phases.append(sc)
            check_done()

            def plan_rope_prep(sc, variant):
                FL = QKV_FLAGS
                rows_blk = 1024 if variant == "a" else 512
                tiles_per_blk = 8 if variant == "a" else 4
                src_ln = xln_a if variant == "a" else xln
                with contextlib.ExitStack() as qs:
                    qec = qs.enter_context
                    if variant == "a":
                        wq_d, wk_d, wv_d = (wbf["wq_a_t"][1], wbf["wk_a_t"][1],
                                            wbf["wv_a_t"][1])
                    else:
                        wq_d, wk_d, wv_d = (wbf["wq_b"][1], wbf["wk_b"][1],
                                            wbf["wv_b"][1])
                    t_w = qec(nc.sbuf_tensor(U("qp_w"), [128, 16 * 3072], BF16))

                    def wq_t(k):
                        return t_w[:, k * 3072:k * 3072 + 2048]

                    def wk_t(k):
                        return t_w[:, k * 3072 + 2048:k * 3072 + 2560]

                    def wv_t(k):
                        return t_w[:, k * 3072 + 2560:k * 3072 + 3072]
                    for k in range(16):
                        dma(sc, "gpsimd", cc, wq_t(k),
                            wq_d[k * 128:(k + 1) * 128, :])
                        dma(sc, "gpsimd", cc, wk_t(k),
                            wk_d[k * 128:(k + 1) * 128, :])
                        dma(sc, "gpsimd", cc, wv_t(k),
                            wv_d[k * 128:(k + 1) * 128, :])
                    fw_qkv = cflag(sc)
                    ps_q2 = [qec(nc.psum_tensor(U("qp_psq"), [128, 1024],
                                                F32)) for _ in range(2)]
                    ps_kv2 = [qec(nc.psum_tensor(U("qp_pskv"), [128, 1024],
                                                 F32)) for _ in range(2)]
                    if variant == "b":
                        t_bq = qec(nc.sbuf_tensor(U("qp_bq"), [128, 2048], F32))
                        t_bkv = qec(nc.sbuf_tensor(U("qp_bkv"), [128, 1024], F32))
                        t_br = qec(nc.sbuf_tensor(U("qp_br"), [1, 2048], F32))
                        t_br2 = qec(nc.sbuf_tensor(U("qp_br2"), [1, 1024], F32))
                        ps_b = ps_q2[0]
                        dma(sc, "sync", ld, t_br[:, :], bq_b[:, :])
                        dma(sc, "sync", ld, t_br2[:, 0:512], bk_b[:, :])
                        dma(sc, "sync", ld, t_br2[:, 512:1024], bv_b[:, :])
                        flb = lflag(sc)
                        sc.wait("tensor", ldf, flb)
                        for j, (dst, row) in enumerate(
                                [(t_bq[:, jj * 512:(jj + 1) * 512],
                                  t_br[:, jj * 512:(jj + 1) * 512])
                                 for jj in range(4)] +
                                [(t_bkv[:, jj * 512:(jj + 1) * 512],
                                  t_br2[:, jj * 512:(jj + 1) * 512])
                                 for jj in range(2)]):
                            nm = sc.on("tensor", lambda eng, r=row: eng.matmul(
                                ps_b[:, 0:512], sb_ones_f32[:, :], r,
                                start=True, stop=True), inc=mm_s)
                            sc.wait("vector", mm_s, nm)
                            nv = sc.on("vector", lambda eng, d=dst:
                                       eng.tensor_copy(d, ps_b[:, 0:512]),
                                       inc=vec)
                            sc.wait("tensor", vec, nv)

                    t_xT = [qec(nc.sbuf_tensor(U(f"qp_xT{i}"),
                                               [128, 16 * rows_blk], BF16))
                            for i in range(2)]
                    t_cq = qec(nc.sbuf_tensor(U("qp_cq"), [128, 64], F32))
                    t_sq_ = qec(nc.sbuf_tensor(U("qp_sq"), [128, 64], F32))
                    t_ck = qec(nc.sbuf_tensor(U("qp_ck"), [128, 64], F32))
                    t_sk = qec(nc.sbuf_tensor(U("qp_sk"), [128, 64], F32))
                    t_qf = qec(nc.sbuf_tensor(U("qp_qf"), [128, 2048], F32))
                    t_kf = qec(nc.sbuf_tensor(U("qp_kf"), [128, 1024], F32))
                    t_rr = qec(nc.sbuf_tensor(U("qp_rr"), [128, 2048], F32))
                    t_rk = qec(nc.sbuf_tensor(U("qp_rk"), [128, 512], F32))
                    t_sqs = qec(nc.sbuf_tensor(U("qp_sqs"), [128, 2048], F32))
                    t_ss = qec(nc.sbuf_tensor(U("qp_ss"), [128, 20], F32))
                    t_sd = qec(nc.sbuf_tensor(U("qp_sd"), [128, 20], F32))
                    t_rst = qec(nc.sbuf_tensor(U("qp_rst"), [128, 20], F32))
                    t_obf = qec(nc.sbuf_tensor(U("qp_obf"), [128, 2048], BF16))
                    t_kvbf = qec(nc.sbuf_tensor(U("qp_kvbf"), [128, 1024], BF16))

                    xT_guard = [0, 0]
                    kv_guard = [0, 0]
                    q_guard = [0, 0]
                    sc.wait("tensor", ccf, fw_qkv)
                    for blk in range(min(NB, FL["nblk"])):
                        xT = t_xT[blk % 2]
                        sc.wait("sync", mm_s, xT_guard[blk % 2])
                        for k in range(16):
                            dma(sc, "sync", ld,
                                xT[:, k * rows_blk:(k + 1) * rows_blk],
                                src_ln[blk * rows_blk:(blk + 1) * rows_blk,
                                       k * 128:(k + 1) * 128],
                                transpose=True)
                        fl_xT = lflag(sc)
                        for sub in range(tiles_per_blk):
                            row0 = blk * rows_blk + sub * 128
                            is_own = (variant == "b") or (sub >= 4)
                            own_row0 = (blk * 512 + (sub - 4) * 128
                                        if variant == "a" else row0)
                            ps_kv = ps_kv2[sub % 2]
                            sc.wait("tensor", ldf, fl_xT)
                            if kv_guard[sub % 2]:
                                sc.wait("tensor", vec, kv_guard[sub % 2])
                            for k in range(16):
                                for half in range(2):
                                    nmkv = sc.on("tensor", lambda eng, k=k,
                                                 h=half, xT=xT, sub=sub,
                                                 ps_kv=ps_kv:
                                                 eng.matmul(
                                                     ps_kv[:, h * 512:
                                                           (h + 1) * 512],
                                                     xT[:, k * rows_blk +
                                                        sub * 128:
                                                        k * rows_blk +
                                                        sub * 128 + 128],
                                                     (wk_t(k) if h == 0
                                                      else wv_t(k)),
                                                     start=(k == 0),
                                                     stop=(k == 15)),
                                                 inc=(mm_s if k == 15 and
                                                      half == 1 else None))
                            nmq = [None, None]
                            if is_own:
                                for qc in range(2):
                                    psq = ps_q2[qc]
                                    if q_guard[qc]:
                                        sc.wait("tensor", vec, q_guard[qc])
                                    for k in range(16):
                                        for j in range(2):
                                            nmq[qc] = sc.on(
                                                "tensor", lambda eng, k=k, j=j,
                                                qc=qc, xT=xT, sub=sub, psq=psq:
                                                eng.matmul(
                                                    psq[:, j * 512:
                                                        (j + 1) * 512],
                                                    xT[:, k * rows_blk +
                                                       sub * 128:
                                                       k * rows_blk + sub * 128
                                                       + 128],
                                                    wq_t(k)[:, qc * 1024 +
                                                            j * 512:
                                                            qc * 1024 + j * 512
                                                            + 512],
                                                    start=(k == 0),
                                                    stop=(k == 15)),
                                                inc=(mm_s if k == 15 and j == 1
                                                     else None))

                            if variant == "a":
                                ck0 = cos_ak[row0:row0 + 128, :]
                                sk0 = sin_ak[row0:row0 + 128, :]
                            else:
                                ck0 = cos_b[row0:row0 + 128, :]
                                sk0 = sin_b[row0:row0 + 128, :]
                            sc.wait("sync", vec, vec.n)
                            dma(sc, "sync", ld, t_ck[:, :], ck0)
                            dma(sc, "sync", ld, t_sk[:, :], sk0)
                            if is_own and variant == "a":
                                dma(sc, "sync", ld, t_cq[:, :],
                                    cos_aq[own_row0:own_row0 + 128, :])
                                dma(sc, "sync", ld, t_sq_[:, :],
                                    sin_aq[own_row0:own_row0 + 128, :])
                            fl_rope = lflag(sc)

                            sc.wait("vector", mm_s, nmkv)
                            sc.wait("vector", ldf, fl_rope)
                            nst = st.n
                            if not FL["do_kv"]:
                                pass
                            elif variant == "a":
                                sc.on("vector", lambda eng, ps_kv=ps_kv:
                                      eng.tensor_copy(
                                          t_kf[:, 0:512], ps_kv[:, 0:512]))
                                nvk = sc.on("vector", lambda eng, ps_kv=ps_kv:
                                            eng.tensor_copy(
                                                t_kf[:, 512:1024],
                                                ps_kv[:, 512:1024]), inc=vec)
                                sc.wait("scalar", vec, nvk)
                                for hh in range(4):
                                    sc.on("scalar", lambda eng, hh=hh:
                                          eng.activation(
                                              t_sqs[:, hh * 128:(hh + 1) * 128],
                                              t_kf[:, hh * 128:(hh + 1) * 128],
                                              AF.Square,
                                              accum_out=t_ss[:, 16 + hh:17 + hh]))
                                sc.chain("scalar", act)
                                nak = sc.on("scalar", lambda eng: eng.activation(
                                    t_sd[:, 16:20], t_ss[:, 16:20], AF.Sqrt,
                                    scale=1.0 / 128.0, bias=RMS_EPS_QK), inc=act)
                                sc.wait("vector", act, nak)
                                sc.on("vector", lambda eng: eng.reciprocal(
                                    t_rst[:, 16:20], t_sd[:, 16:20]))
                                sc.chain("vector", vec)
                                for hh in range(4):
                                    b0 = hh * 128
                                    x1a = t_kf[:, b0:b0 + 64]
                                    x2a = t_kf[:, b0 + 64:b0 + 128]
                                    o1 = t_rk[:, b0:b0 + 64]
                                    o2 = t_rk[:, b0 + 64:b0 + 128]
                                    sc.on("vector", lambda eng, a=x1a, o=o1:
                                          eng.tensor_tensor(o, a, t_ck[:, :],
                                                            ALU.mult))
                                    sc.on("vector", lambda eng, a=x2a, o=o2:
                                          eng.tensor_tensor(o, a, t_sk[:, :],
                                                            ALU.mult))
                                    sc.on("vector", lambda eng, o=o1, o2=o2:
                                          eng.tensor_tensor(o, o, o2, ALU.add))
                                    sc.on("vector", lambda eng, a=x2a, o=o2:
                                          eng.tensor_tensor(o, a, t_ck[:, :],
                                                            ALU.mult))
                                    sc.on("vector", lambda eng, a=x1a, s=x2a:
                                          eng.tensor_tensor(s, a, t_sk[:, :],
                                                            ALU.mult))
                                    sc.on("vector", lambda eng, o=o2, s=x2a:
                                          eng.tensor_tensor(o, o, s,
                                                            ALU.subtract))
                                    sc.on("vector", lambda eng, hh=hh, b0=b0:
                                          eng.tensor_scalar(
                                              t_kvbf[:, b0:b0 + 128],
                                              t_rk[:, b0:b0 + 128],
                                              t_rst[:, 16 + hh:17 + hh], None,
                                              ALU.mult))
                                nvv = sc.on("vector", lambda eng,
                                            ps_kv=ps_kv: eng.tensor_copy(
                                                t_kvbf[:, 512:1024],
                                                ps_kv[:, 512:1024]), inc=vec)
                                kv_guard[sub % 2] = nvv
                                sc.wait("sync", vec, nvv)
                                dma(sc, "sync", st, ka_buf[row0:row0 + 128, :],
                                    t_kvbf[:, 0:512])
                                nst = dma(sc, "sync", st,
                                          va_buf[row0:row0 + 128, :],
                                          t_kvbf[:, 512:1024])
                            else:
                                for jq in range(2):
                                    sc.on("vector", lambda eng, jq=jq,
                                          ps_kv=ps_kv: eng.tensor_tensor(
                                              t_kf[:, jq * 512:(jq + 1) * 512],
                                              ps_kv[:, jq * 512:(jq + 1) * 512],
                                              t_bkv[:, jq * 512:(jq + 1) * 512],
                                              ALU.add))
                                for hh in range(4):
                                    b0 = hh * 128
                                    xr = t_kf[:, b0:b0 + 128:2]
                                    xi = t_kf[:, b0 + 1:b0 + 128:2]
                                    orr = t_rk[:, b0:b0 + 128:2]
                                    oi = t_rk[:, b0 + 1:b0 + 128:2]
                                    q1 = t_sqs[:, 0:64]
                                    q2 = t_sqs[:, 64:128]
                                    sc.on("vector", lambda eng, a=xr, o=q1:
                                          eng.tensor_tensor(o, a, t_ck[:, :],
                                                            ALU.mult))
                                    sc.on("vector", lambda eng, a=xi, o=q2:
                                          eng.tensor_tensor(o, a, t_sk[:, :],
                                                            ALU.mult))
                                    sc.on("vector", lambda eng, o=orr, a=q1,
                                          b2=q2: eng.tensor_tensor(
                                              o, a, b2, ALU.subtract))
                                    sc.on("vector", lambda eng, a=xr, o=q1:
                                          eng.tensor_tensor(o, a, t_sk[:, :],
                                                            ALU.mult))
                                    sc.on("vector", lambda eng, a=xi, o=q2:
                                          eng.tensor_tensor(o, a, t_ck[:, :],
                                                            ALU.mult))
                                    sc.on("vector", lambda eng, o=oi, a=q1,
                                          b2=q2: eng.tensor_tensor(
                                              o, a, b2, ALU.add))
                                sc.on("vector", lambda eng: eng.tensor_copy(
                                    t_kvbf[:, 0:512], t_rk[:, :]))
                                nvv = sc.on("vector", lambda eng,
                                            ps_kv=ps_kv: eng.tensor_tensor(
                                                t_kvbf[:, 512:1024],
                                                ps_kv[:, 512:1024],
                                                t_bkv[:, 512:1024], ALU.add),
                                            inc=vec)
                                kv_guard[sub % 2] = nvv
                                sc.wait("sync", vec, nvv)
                                dma(sc, "sync", st, kb_own[row0:row0 + 128, :],
                                    t_kvbf[:, 0:512])
                                nst = dma(sc, "sync", st,
                                          vb_own[row0:row0 + 128, :],
                                          t_kvbf[:, 512:1024])

                            if is_own and FL["do_q"]:
                                if variant == "a":
                                    for qc in range(2):
                                        sc.wait("vector", mm_s, nmq[qc])
                                        for jq in range(2):
                                            nvq = sc.on(
                                                "vector", lambda eng, jq=jq,
                                                qc=qc: eng.tensor_copy(
                                                    t_qf[:, qc * 1024 + jq * 512:
                                                         qc * 1024 + jq * 512
                                                         + 512],
                                                    ps_q2[qc][:, jq * 512:
                                                              (jq + 1) * 512]),
                                                inc=(vec if jq == 1 else None))
                                        q_guard[qc] = nvq
                                    sc.wait("scalar", vec, nvq)
                                    for hh in range(16):
                                        sc.on("scalar", lambda eng, hh=hh:
                                              eng.activation(
                                                  t_sqs[:, hh * 128:
                                                        (hh + 1) * 128],
                                                  t_qf[:, hh * 128:
                                                       (hh + 1) * 128],
                                                  AF.Square,
                                                  accum_out=t_ss[:, hh:hh + 1]))
                                    sc.chain("scalar", act)
                                    naq = sc.on("scalar", lambda eng:
                                                eng.activation(
                                                    t_sd[:, 0:16], t_ss[:, 0:16],
                                                    AF.Sqrt, scale=1.0 / 128.0,
                                                    bias=RMS_EPS_QK), inc=act)
                                    sc.wait("vector", act, naq)
                                    sc.on("vector", lambda eng: eng.reciprocal(
                                        t_rst[:, 0:16], t_sd[:, 0:16]))
                                    sc.chain("vector", vec)
                                    for hh in range(16):
                                        b0 = hh * 128
                                        x1a = t_qf[:, b0:b0 + 64]
                                        x2a = t_qf[:, b0 + 64:b0 + 128]
                                        o1 = t_rr[:, b0:b0 + 64]
                                        o2 = t_rr[:, b0 + 64:b0 + 128]
                                        sc.on("vector", lambda eng, a=x1a, o=o1:
                                              eng.tensor_tensor(
                                                  o, a, t_cq[:, :], ALU.mult))
                                        sc.on("vector", lambda eng, a=x2a, o=o2:
                                              eng.tensor_tensor(
                                                  o, a, t_sq_[:, :], ALU.mult))
                                        sc.on("vector", lambda eng, o=o1, o2=o2:
                                              eng.tensor_tensor(o, o, o2,
                                                                ALU.add))
                                        sc.on("vector", lambda eng, a=x2a, o=o2:
                                              eng.tensor_tensor(
                                                  o, a, t_cq[:, :], ALU.mult))
                                        sc.on("vector", lambda eng, a=x1a, s=x2a:
                                              eng.tensor_tensor(
                                                  s, a, t_sq_[:, :], ALU.mult))
                                        sc.on("vector", lambda eng, o=o2, s=x2a:
                                              eng.tensor_tensor(o, o, s,
                                                                ALU.subtract))
                                        sc.on("vector", lambda eng, hh=hh, b0=b0:
                                              eng.tensor_scalar(
                                                  t_obf[:, b0:b0 + 128],
                                                  t_rr[:, b0:b0 + 128],
                                                  t_rst[:, hh:hh + 1], None,
                                                  ALU.mult),
                                              inc=(vec if hh == 15 else None))
                                else:
                                    for qc in range(2):
                                        sc.wait("vector", mm_s, nmq[qc])
                                        for jq in range(2):
                                            nvq = sc.on(
                                                "vector", lambda eng, jq=jq,
                                                qc=qc: eng.tensor_tensor(
                                                    t_qf[:, qc * 1024 + jq * 512:
                                                         qc * 1024 + jq * 512
                                                         + 512],
                                                    ps_q2[qc][:, jq * 512:
                                                              (jq + 1) * 512],
                                                    t_bq[:, qc * 1024 + jq * 512:
                                                         qc * 1024 + jq * 512
                                                         + 512],
                                                    ALU.add),
                                                inc=(vec if jq == 1 else None))
                                        q_guard[qc] = nvq
                                    for hh in range(16):
                                        b0 = hh * 128
                                        xr = t_qf[:, b0:b0 + 128:2]
                                        xi = t_qf[:, b0 + 1:b0 + 128:2]
                                        orr = t_rr[:, b0:b0 + 128:2]
                                        oi = t_rr[:, b0 + 1:b0 + 128:2]
                                        q1 = t_sqs[:, 0:64]
                                        q2 = t_sqs[:, 64:128]
                                        sc.on("vector", lambda eng, a=xr, o=q1:
                                              eng.tensor_tensor(
                                                  o, a, t_ck[:, :], ALU.mult))
                                        sc.on("vector", lambda eng, a=xi, o=q2:
                                              eng.tensor_tensor(
                                                  o, a, t_sk[:, :], ALU.mult))
                                        sc.on("vector", lambda eng, o=orr, a=q1,
                                              b2=q2: eng.tensor_tensor(
                                                  o, a, b2, ALU.subtract))
                                        sc.on("vector", lambda eng, a=xr, o=q1:
                                              eng.tensor_tensor(
                                                  o, a, t_sk[:, :], ALU.mult))
                                        sc.on("vector", lambda eng, a=xi, o=q2:
                                              eng.tensor_tensor(
                                                  o, a, t_ck[:, :], ALU.mult))
                                        sc.on("vector", lambda eng, o=oi, a=q1,
                                              b2=q2: eng.tensor_tensor(
                                                  o, a, b2, ALU.add))
                                    nvq2 = sc.on("vector", lambda eng:
                                                 eng.tensor_copy(
                                                     t_obf[:, :], t_rr[:, :]),
                                                 inc=vec)
                                nvq2 = vec.n
                                sc.wait("sync", vec, nvq2)
                                qdst = qa if variant == "a" else qb
                                nst = dma(sc, "sync", st,
                                          qdst[own_row0:own_row0 + 128, :],
                                          t_obf[:, :])
                            fs_t = sflag(sc)
                            sc.wait("vector", stf, fs_t)
                        xT_guard[blk % 2] = mm_s.n
                    sc.wait("sync", st, st.n)

            sc = Script()
            start_next(sc)
            sc.wait("sync", cw, cw.n)
            plan_rope_prep(sc, "a")
            start_next = barrier(sc)
            phases.append(sc)
            check_done()

            def plan_attn(sc, variant):
                qsrc = qa if variant == "a" else qb
                ksrc = ka_buf if variant == "a" else ag_k
                vsrc = va_buf if variant == "a" else ag_v
                msrc = mask_a if variant == "a" else mask_b
                wo_d = wbf["wo_a_t"][1] if variant == "a" else wbf["wo_b"][1]
                xin = x_own if variant == "a" else x2
                xout = x1 if variant == "a" else x3
                kcols = NB * 1024 if variant == "a" else 2 * TPC
                with contextlib.ExitStack() as ats:
                    aec = ats.enter_context
                    t_kT = aec(nc.sbuf_tensor(U("at_kT"), [128, 4 * kcols],
                                              BF16))
                    t_v = aec(nc.sbuf_tensor(
                        U("at_v"), [128, (kcols // 128) * 512], BF16))
                    t_wo = aec(nc.sbuf_tensor(U("at_wo"), [128, 16 * 2048],
                                              BF16))
                    t_mask = aec(nc.sbuf_tensor(U("at_mask"), [128, 8 * 512],
                                                BF16))
                    t_qT = aec(nc.sbuf_tensor(U("at_qT"), [128, 2 * 512],
                                              BF16))
                    t_p = aec(nc.sbuf_tensor(U("at_p"), [128, 12 * 512], BF16))
                    t_rl = aec(nc.sbuf_tensor(U("at_rl"), [128, 512], F32))
                    t_oall = aec(nc.sbuf_tensor(U("at_oall"),
                                                [128, 16 * 512], BF16))
                    t_xold2 = [aec(nc.sbuf_tensor(U("at_xold"),
                                                  [128, 2048], F32))
                               for _ in range(2)]
                    t_xnew2 = [aec(nc.sbuf_tensor(U("at_xnew"),
                                                  [128, 2048], F32))
                               for _ in range(2)]
                    t_bo = (aec(nc.sbuf_tensor(U("at_bo"), [128, 2048],
                                                BF16))
                            if variant == "b" else None)
                    ps_s = [aec(nc.psum_tensor(U("at_ps_s"), [128, 1024],
                                               F32)) for _ in range(2)]
                    ps_l = aec(nc.psum_tensor(U("at_ps_l"), [128, 512], F32))
                    ps_o = aec(nc.psum_tensor(U("at_ps_o"), [128, 512], F32))
                    ps_w = aec(nc.psum_tensor(U("at_ps_w"), [128, 1024], F32))

                    for kv in range(4):
                        for c0 in range(0, kcols, 2048):
                            w = min(2048, kcols - c0)
                            dma(sc, "sync", ld,
                                t_kT[:, kv * kcols + c0:kv * kcols + c0 + w],
                                ksrc[c0:c0 + w, kv * 128:(kv + 1) * 128],
                                transpose=True)
                    for r0 in range(0, kcols, 512):
                        v0 = vsrc[r0:r0 + 512, :]
                        vsrc_r = AP(v0.tensor, v0.offset,
                                    [[512, 128], [512 * 128, 4], [1, 512]])
                        dma(sc, "sync", ld,
                            t_v[:, (r0 // 128) * 512:
                                (r0 // 128) * 512 + 4 * 512],
                            vsrc_r)
                    for k in range(16):
                        dma(sc, "gpsimd", cc,
                            t_wo[:, k * 2048:(k + 1) * 2048],
                            wo_d[k * 128:(k + 1) * 128, :])
                    fw_wo = cflag(sc)
                    fl_kv = lflag(sc)
                    if variant == "b":
                        if True:
                            t_bor = t_xnew2[1][0:1, 0:2048]
                            dma(sc, "sync", ld, t_bor, bo_b[:, :])
                            flbo = lflag(sc)
                            sc.wait("tensor", ldf, flbo)
                            for j in range(4):
                                nm = sc.on("tensor", lambda eng, j=j:
                                           eng.matmul(
                                               ps_w[:, 0:512],
                                               sb_ones_f32[:, :],
                                               t_bor[:, j * 512:
                                                     (j + 1) * 512],
                                               start=True, stop=True),
                                           inc=mm_s)
                                sc.wait("vector", mm_s, nm)
                                nv = sc.on("vector", lambda eng, j=j:
                                           eng.tensor_copy(
                                               t_bo[:, j * 512:
                                                    (j + 1) * 512],
                                               ps_w[:, 0:512]), inc=vec)
                                sc.wait("tensor", vec, nv)

                    def kpos(kt):
                        if variant == "a":
                            return kt * 128
                        g, sub = divmod(kt, 4)
                        return (g % 2) * 2048 + (g // 2) * 512 + sub * 128

                    qT_guard = [0, 0]
                    for blk in range(NB):
                        sc.wait("sync", vec, vec.n)
                        for sl in range(8):
                            dma(sc, "sync", ld,
                                t_mask[:, sl * 512:(sl + 1) * 512],
                                msrc[blk, sl, :, :])
                        fl_mk = lflag(sc)
                        if variant == "a":
                            ktiles = [blk * 8 + j for j in range(8)]
                            mask_slot = {blk * 8 + j: j for j in range(8)}
                        else:
                            ktiles = list(range(8 * blk + 8))
                            mask_slot = {8 * blk + j: j for j in range(8)}
                        pairs = [ktiles[i:i + 2]
                                 for i in range(0, len(ktiles), 2)]
                        sc.wait("vector", mm_s, mm_s.n)
                        qflags = {}

                        def load_q(h2):
                            qb2 = h2 % 2
                            dma(sc, "sync", ld,
                                t_qT[:, qb2 * 512:(qb2 + 1) * 512],
                                qsrc[blk * 512:(blk + 1) * 512,
                                     h2 * 128:(h2 + 1) * 128],
                                transpose=True)
                            qflags[h2] = lflag(sc)

                        load_q(0)
                        nexp = [0, 0, 0]
                        ncons = [0, 0, 0]
                        npool = {}
                        nvec_evac = 0
                        for hh in range(16):
                            kv = hh // GRP
                            qb_i = hh % 2
                            if True:
                                if hh + 1 < 16:
                                    sc.wait("sync", mm_s,
                                            qT_guard[(hh + 1) % 2])
                                    load_q(hh + 1)
                                sc.wait("tensor", ldf, max(qflags[hh],
                                                           fl_kv, fl_mk))
                                qs_ap = t_qT[:, qb_i * 512:(qb_i + 1) * 512]
                                ROT = 3

                                def spsum_of(pi):
                                    return (ps_s[0], ps_s[1], ps_w)[pi % ROT]

                                def pslot_of(pi, ji):
                                    return t_p[:, ((pi % ROT) * 2 + ji) * 512:
                                               ((pi % ROT) * 2 + ji) * 512
                                               + 512]

                                def mslot_of(pi, ji):
                                    return t_p[:, (6 + (pi % ROT) * 2 + ji)
                                               * 512:
                                               (6 + (pi % ROT) * 2 + ji)
                                               * 512 + 512]

                                state = {}

                                def emit_S(pi):
                                    pair = pairs[pi]
                                    sp = spsum_of(pi)
                                    pk = pi - ROT
                                    if pk >= 0 and ("exp", pk) in state:
                                        sc.wait("tensor", act,
                                                state[("exp", pk)])
                                    elif pk < 0 and nexp[pi % ROT]:
                                        sc.wait("tensor", act,
                                                nexp[pi % ROT])
                                    for ji, kt in enumerate(pair):
                                        kc = kpos(kt)
                                        nm = sc.on(
                                            "tensor", lambda eng, kv=kv,
                                            kc=kc, ji=ji, sp=sp,
                                            qs_ap=qs_ap: eng.matmul(
                                                sp[:, ji * 512:
                                                   (ji + 1) * 512],
                                                t_kT[:, kv * kcols + kc:
                                                     kv * kcols + kc + 128],
                                                qs_ap, start=True,
                                                stop=True),
                                            inc=mm_s)
                                    state[("S", pi)] = nm

                                def emit_exp(pi):
                                    pair = pairs[pi]
                                    sp = spsum_of(pi)
                                    wexp = 512 * len(pair)
                                    psl = t_p[:, (pi % ROT) * 1024:
                                              (pi % ROT) * 1024 + wexp]
                                    sc.wait("scalar", mm_s, state[("S", pi)])
                                    pk = pi - ROT
                                    if pk >= 0 and ("av", pk) in state:
                                        sc.wait("scalar", mm_s,
                                                state[("av", pk)])
                                        sc.wait("scalar", pl,
                                                state.get(("pool", pk), 0))
                                    elif pk < 0 and ncons[pi % ROT]:
                                        sc.wait("scalar", mm_s,
                                                ncons[pi % ROT])
                                        sc.wait("scalar", pl,
                                                npool.get(pi % ROT, 0))
                                    if variant == "a":
                                        na = sc.on(
                                            "scalar",
                                            lambda eng, hh=hh, sp=sp, p=psl,
                                            w=wexp: eng.activation(
                                                p, sp[:, 0:w], AF.Exp,
                                                bias=sb_gbias[:, hh:hh + 1],
                                                scale=sb_gsc[:, hh:hh + 1]),
                                            inc=act)
                                    else:
                                        na = sc.on(
                                            "scalar",
                                            lambda eng, sp=sp, p=psl,
                                            w=wexp: eng.activation(
                                                p, sp[:, 0:w], AF.Exp,
                                                bias=-SHIFT_B), inc=act)
                                    state[("exp", pi)] = na
                                    mx = 0
                                    uses = []
                                    for ji, kt in enumerate(pair):
                                        if kt in mask_slot:
                                            sl_ = mask_slot[kt]
                                            m = mslot_of(pi, ji)
                                            p0 = pslot_of(pi, ji)
                                            sc.wait("gpsimd", act, na)
                                            mx = sc.on(
                                                "gpsimd",
                                                lambda eng, p=p0, m=m,
                                                sl_=sl_: eng.tensor_tensor(
                                                    m, p,
                                                    t_mask[:, sl_ * 512:
                                                           (sl_ + 1) * 512],
                                                    ALU.mult), inc=pl)
                                            uses.append(m)
                                        else:
                                            uses.append(pslot_of(pi, ji))
                                    state[("pool", pi)] = mx
                                    state[("use", pi)] = uses

                                def emit_lav(pi):
                                    pair = pairs[pi]
                                    if state.get(("pool", pi)):
                                        sc.wait("tensor", pl,
                                                state[("pool", pi)])
                                    else:
                                        sc.wait("tensor", act,
                                                state[("exp", pi)])
                                    if pi == 0 and nvec_evac:
                                        sc.wait("tensor", vec, nvec_evac)
                                    for ji, kt in enumerate(pair):
                                        kc = kpos(kt)
                                        pap = state[("use", pi)][ji]
                                        f = (pi == 0 and ji == 0)
                                        la = (kt == ktiles[-1])
                                        sc.on("tensor", lambda eng, p=pap,
                                              f=f, la=la: eng.matmul(
                                                  ps_l[:, :],
                                                  sb_ones[:, :], p,
                                                  start=f, stop=la))
                                        nc2 = sc.on(
                                            "tensor", lambda eng, p=pap,
                                            kv=kv, kc=kc, f=f, la=la:
                                            eng.matmul(
                                                ps_o[:, :],
                                                t_v[:, (kc // 128) * 512 +
                                                    kv * 128:
                                                    (kc // 128) * 512 +
                                                    kv * 128 + 128],
                                                p, start=f, stop=la),
                                            inc=mm_s)
                                    state[("av", pi)] = nc2

                                np_ = len(pairs)
                                emit_S(0)
                                if np_ > 1:
                                    emit_S(1)
                                emit_exp(0)
                                for pi in range(np_):
                                    if pi + 2 < np_:
                                        emit_S(pi + 2)
                                    if pi + 1 < np_:
                                        emit_exp(pi + 1)
                                    emit_lav(pi)
                                nexp = [0, 0, 0]
                                ncons = [0, 0, 0]
                                npool = {}
                                for pi in range(max(0, np_ - ROT), np_):
                                    nexp[pi % ROT] = state[("exp", pi)]
                                    ncons[pi % ROT] = state[("av", pi)]
                                    npool[pi % ROT] = state.get(("pool", pi),
                                                                0)
                                qT_guard[qb_i] = mm_s.n
                                nmm_done = mm_s.n
                                sc.wait("vector", mm_s, nmm_done)
                                sc.on("vector", lambda eng: eng.reciprocal(
                                    t_rl[:, :], ps_l[:, :]))
                                if variant == "b":
                                    sc.on("vector", lambda eng:
                                          eng.tensor_scalar(
                                              t_rl[:, :], t_rl[:, :],
                                              2048.0 ** -0.5, None,
                                              ALU.mult))
                                sc.chain("vector", vec)
                                nvec_evac = sc.on(
                                    "vector", lambda eng, hh=hh:
                                    eng.tensor_tensor(
                                        t_oall[:, hh * 512:(hh + 1) * 512],
                                        ps_o[:, :], t_rl[:, :],
                                        ALU.mult), inc=vec)
                        sc.wait("tensor", vec, vec.n)
                        sc.wait("tensor", ccf, fw_wo)
                        wo_ev = [0, 0]
                        xo_flag = {}

                        def load_xold(q2):
                            dma(sc, "sync", ld, t_xold2[q2 % 2][:, :],
                                xin[blk * 512 + q2 * 128:
                                    blk * 512 + q2 * 128 + 128, :])
                            xo_flag[q2] = lflag(sc)

                        load_xold(0)
                        for qs_i in range(4):
                            row0 = blk * 512 + qs_i * 128
                            if qs_i + 1 < 4:
                                load_xold(qs_i + 1)

                            def wchunk(e, qs_i=qs_i):
                                if qs_i % 2 == 0:
                                    return (ps_s[0][:, e * 512:
                                                    (e + 1) * 512]
                                            if e < 2 else
                                            ps_w[:, (e - 2) * 512:
                                                 (e - 2) * 512 + 512])
                                return (ps_s[1][:, e * 512:(e + 1) * 512]
                                        if e < 2 else
                                        (ps_l[:, :] if e == 2
                                         else ps_o[:, :]))
                            if wo_ev[qs_i % 2]:
                                sc.wait("tensor", vec, wo_ev[qs_i % 2])
                            for e in range(4):
                                tgt = wchunk(e)
                                for hh2 in range(16):
                                    nmw = sc.on(
                                        "tensor", lambda eng, hh2=hh2, e=e,
                                        qs_i=qs_i, tgt=tgt: eng.matmul(
                                            tgt,
                                            t_oall[:, hh2 * 512 +
                                                   qs_i * 128:
                                                   hh2 * 512 + qs_i * 128
                                                   + 128],
                                            t_wo[:, hh2 * 2048 + e * 512:
                                                 hh2 * 2048 + e * 512
                                                 + 512],
                                            start=(hh2 == 0),
                                            stop=(hh2 == 15)),
                                        inc=(mm_s if hh2 == 15 else None))
                            sc.wait("vector", mm_s, nmw)
                            sc.wait("vector", ldf, xo_flag[qs_i])
                            sc.wait("vector", stf, stf.n)
                            xoq = t_xold2[qs_i % 2]
                            xnq = t_xnew2[qs_i % 2]
                            for e in range(4):
                                tgt = wchunk(e)
                                if variant == "b":
                                    sc.on("vector", lambda eng, e=e,
                                          tgt=tgt, xnq=xnq:
                                          eng.tensor_tensor(
                                              xnq[:, e * 512:(e + 1) * 512],
                                              tgt,
                                              t_bo[:, e * 512:
                                                   (e + 1) * 512],
                                              ALU.add))
                                    nvw = sc.on(
                                        "vector", lambda eng, e=e, xnq=xnq,
                                        xoq=xoq: eng.tensor_tensor(
                                            xnq[:, e * 512:(e + 1) * 512],
                                            xnq[:, e * 512:(e + 1) * 512],
                                            xoq[:, e * 512:(e + 1) * 512],
                                            ALU.add),
                                        inc=(vec if e == 3 else None))
                                else:
                                    nvw = sc.on(
                                        "vector", lambda eng, e=e, tgt=tgt,
                                        xnq=xnq, xoq=xoq:
                                        eng.tensor_tensor(
                                            xnq[:, e * 512:(e + 1) * 512],
                                            tgt,
                                            xoq[:, e * 512:(e + 1) * 512],
                                            ALU.add),
                                        inc=(vec if e == 3 else None))
                            wo_ev[qs_i % 2] = nvw
                            sc.wait("sync", vec, nvw)
                            dma(sc, "sync", st, xout[row0:row0 + 128, :],
                                xnq[:, :])
                            sflag(sc)
                    sc.wait("sync", st, st.n)

            sc = Script()
            start_next(sc)
            sc.wait("sync", cw, cw.n)
            for s_, d_ in ((w_gu_sh, w_gu), (w_dn_sh, w_dn)):
                sc.on("gpsimd", lambda eng, s=s_, d=d_:
                      eng.collective_compute(
                          "AllGather", ALU.bypass,
                          replica_groups=[list(range(8))],
                          ins=[s.opt()], outs=[d.opt()]), inc=col)
            plan_attn(sc, "a")
            start_next = barrier(sc)
            phases.append(sc)
            check_done()

            sc = Script()
            start_next(sc)
            plan_ln(sc, lambda i: x1[i * 128:(i + 1) * 128, :], xln, 16)
            start_next = barrier(sc)
            phases.append(sc)
            check_done()

            def plan_ffn(sc, xres_ap, xout_ap):
                GROUP = 4
                LW = GROUP * 128
                NGRP = 44 // GROUP
                with contextlib.ExitStack() as fs:
                    fec = fs.enter_context
                    t_xT = fec(nc.sbuf_tensor(U("ff_xT"), [128, 16 * 1024],
                                              BF16))
                    t_wg = [fec(nc.sbuf_tensor(U("ff_wg"),
                                               [128, 16 * 512 * GROUP // 4],
                                               BF16)) for _ in range(2)]
                    t_hh = fec(nc.sbuf_tensor(U("ff_h"), [128, 45 * 1024],
                                              BF16))
                    t_wd = [fec(nc.sbuf_tensor(U("ff_wd"), [128, 22 * 512],
                                               BF16)) for _ in range(2)]
                    t_y = fec(nc.sbuf_tensor(U("ff_y"), [128, 512], F32))
                    t_ss = fec(nc.sbuf_tensor(U("ff_ss"), [128, 4], F32))
                    t_ssum = fec(nc.sbuf_tensor(U("ff_ssum"), [128, 1], F32))
                    t_sd = fec(nc.sbuf_tensor(U("ff_sd"), [128, 1], F32))
                    t_rs = fec(nc.sbuf_tensor(U("ff_rs"), [128, 1], F32))
                    t_sq = fec(nc.sbuf_tensor(U("ff_sq"), [128, 512], BF16))
                    t_ss4 = fec(nc.sbuf_tensor(U("ff_ss4"), [128, 32], F32))
                    t_sm8 = fec(nc.sbuf_tensor(U("ff_sm8"), [128, 8], F32))
                    t_sd8 = fec(nc.sbuf_tensor(U("ff_sd8"), [128, 8], F32))
                    t_rs8 = fec(nc.sbuf_tensor(U("ff_rs8"), [128, 8], F32))
                    t_xc1 = t_wg[1][:, 0:2048].bitcast(F32)
                    t_xc2 = t_wg[1][:, 2048:4096].bitcast(F32)
                    t_y2 = t_wd[0][:, 0:4096].bitcast(F32)
                    t_xo = t_wd[0][:, 4096:8192].bitcast(F32)
                    ps = [fec(nc.psum_tensor(U("ff_ps"), [128, 1024], F32))
                          for _ in range(4)]

                    def hslot(i):
                        return t_hh[:, ((i + 44) % 45) * 1024:
                                    ((i + 44) % 45) * 1024 + 1024]

                    def hprev(i):
                        return t_hh[:, (i % 45) * 1024:(i % 45) * 1024 + 1024]

                    carry = {}
                    for half in range(2):
                        tok0 = half * 1024
                        gflag = {}
                        nmg_hist = {}

                        def load_wgroup(g, gflag=gflag):
                            buf = t_wg[g % 2]
                            row0 = ((g // NGRP) * I_FF
                                    + (g % NGRP) * GROUP * 128)
                            for k in range(16):
                                dma(sc, "sync", ld,
                                    buf[:, k * LW:(k + 1) * LW],
                                    w_gu[row0:row0 + LW,
                                         k * 128:(k + 1) * 128],
                                    transpose=True)
                            gflag[g] = lflag(sc)

                        if half == 0:
                            fence(sc)
                            for k in range(16):
                                dma(sc, "sync", ld,
                                    t_xT[:, k * 1024:(k + 1) * 1024],
                                    xln[tok0:tok0 + 1024,
                                        k * 128:(k + 1) * 128],
                                    transpose=True)
                            load_wgroup(0)
                        else:
                            gflag[0] = carry["g0"]

                        wdflag = {}

                        def load_wd(eq, seg, wdflag=wdflag):
                            buf = t_wd[(2 * eq + seg) % 2]
                            for ii in range(22):
                                i = seg * 22 + ii
                                dma(sc, "sync", ld,
                                    buf[:, ii * 512:(ii + 1) * 512],
                                    w_dn[eq * 512:(eq + 1) * 512,
                                         i * 128:(i + 1) * 128],
                                    transpose=True)
                            wdflag[(eq, seg)] = lflag(sc)

                        for g in range(2 * NGRP):
                            stage = g // NGRP
                            buf = t_wg[g % 2]
                            if g == 2 * NGRP - 2:
                                sc.wait("sync", st, st.n)
                                load_wd(0, 0)
                            if g + 1 < 2 * NGRP and (g + 1) not in gflag:
                                if g - 1 in nmg_hist:
                                    sc.wait("sync", mm_s, nmg_hist[g - 1])
                                if half > 0 and g == 0:
                                    sc.wait("sync", st, st.n)
                                load_wgroup(g + 1)
                            sc.wait("tensor", ldf, gflag[g])
                            if half > 0 and g == 0:
                                sc.wait("tensor", vec, carry["nvy_last"])
                            for ig in range(GROUP):
                                i = (g % NGRP) * GROUP + ig
                                psx = ps[2 * stage + (i % 2)]
                                if stage == 0:
                                    key = ("silu", i - 2)
                                    if key in nmg_hist:
                                        sc.wait("tensor", act, nmg_hist[key])
                                else:
                                    key = ("up", i - 2)
                                    if key in nmg_hist:
                                        sc.wait("tensor", vec, nmg_hist[key])
                                for k in range(16):
                                    for j in range(2):
                                        nmg = sc.on(
                                            "tensor", lambda eng, k=k, j=j,
                                            ig=ig, psx=psx, buf=buf:
                                            eng.matmul(
                                                psx[:, j * 512:(j + 1) * 512],
                                                buf[:, k * LW + ig * 128:
                                                    k * LW + ig * 128 + 128],
                                                t_xT[:, k * 1024 + j * 512:
                                                     k * 1024 + j * 512 + 512],
                                                start=(k == 0),
                                                stop=(k == 15)),
                                            inc=(mm_s if k == 15 and j == 1
                                                 else None))
                                nmg_hist[g] = nmg
                                if stage == 0:
                                    if half > 0 and i == 0:
                                        sc.wait("scalar", mm_s, carry["nmd6"])
                                    if half > 0 and i == 21:
                                        sc.wait("scalar", mm_s, carry["nmd7"])
                                    sc.wait("scalar", mm_s, nmg)
                                    nmg_hist[("silu", i)] = sc.on(
                                        "scalar", lambda eng, i=i, psx=psx:
                                        eng.activation(hprev(i), psx[:, :],
                                                       AF.Silu), inc=act)
                                else:
                                    sc.wait("vector", mm_s, nmg)
                                    for jq in range(2):
                                        nmg_hist[("up", i)] = sc.on(
                                            "vector",
                                            lambda eng, i=i, jq=jq, psx=psx:
                                            eng.tensor_tensor(
                                                hslot(i)[:, jq * 512:
                                                         (jq + 1) * 512],
                                                hprev(i)[:, jq * 512:
                                                         (jq + 1) * 512],
                                                psx[:, jq * 512:
                                                    (jq + 1) * 512],
                                                ALU.mult),
                                            inc=(vec if jq == 1 else None))
                        if half == 0:
                            sc.wait("sync", mm_s, nmg_hist[2 * NGRP - 1])
                            for k in range(16):
                                dma(sc, "sync", ld,
                                    t_xT[:, k * 1024:(k + 1) * 1024],
                                    xln[1024:2048,
                                        k * 128:(k + 1) * 128],
                                    transpose=True)
                            for k in range(16):
                                dma(sc, "sync", ld,
                                    t_wg[0][:, k * LW:(k + 1) * LW],
                                    w_gu[0:LW, k * 128:(k + 1) * 128],
                                    transpose=True)
                            carry["g0"] = lflag(sc)
                        sc.wait("tensor", vec, nmg_hist[("up", 43)])
                        sc.wait("tensor", act, nmg_hist[("silu", 43)])
                        nvy = {}
                        nvya = {}
                        nmd_hist = {}
                        nsq_last = 0
                        tail_stf = 0
                        for eq in range(4):
                            for seg in range(2):
                                buf = t_wd[(2 * eq + seg) % 2]
                                nxt = (eq, 1) if seg == 0 else (eq + 1, 0)
                                if nxt[0] < 4:
                                    idx = 2 * eq + seg
                                    if idx >= 1 and (idx - 1) in nmd_hist:
                                        sc.wait("sync", mm_s,
                                                nmd_hist[idx - 1])
                                    load_wd(*nxt)
                                sc.wait("tensor", ldf, wdflag[(eq, seg)])
                                for ts_i in range(8):
                                    psd = ps[ts_i // 2][
                                        :, (ts_i % 2) * 512:
                                        (ts_i % 2) * 512 + 512]
                                    if seg == 0 and (eq, ts_i) in nvy:
                                        sc.wait("tensor", vec,
                                                nvy.pop((eq, ts_i), 0))
                                    if seg == 0 and (eq - 1, ts_i) in nvy:
                                        sc.wait("tensor", vec,
                                                nvy.pop((eq - 1, ts_i)))
                                    for ii in range(22):
                                        i = seg * 22 + ii
                                        nmd = sc.on(
                                            "tensor", lambda eng, i=i, ii=ii,
                                            ts_i=ts_i, psd=psd, buf=buf:
                                            eng.matmul(
                                                psd,
                                                hslot(i)[:, ts_i * 128:
                                                         ts_i * 128 + 128],
                                                buf[:, ii * 512:
                                                    (ii + 1) * 512],
                                                start=(i == 0),
                                                stop=(i == 43)),
                                            inc=(mm_s if ii == 21 else None))
                                    if seg == 1:
                                        sc.wait("vector", mm_s, nmd)
                                        sc.wait("vector", stf, stf.n)
                                        if nsq_last:
                                            sc.wait("vector", act, nsq_last)
                                        nv = sc.on(
                                            "vector",
                                            lambda eng, psd=psd:
                                            eng.tensor_copy(t_y[:, 0:512],
                                                            psd), inc=vec)
                                        nvy[(eq, ts_i)] = nv
                                        sc.wait("scalar", vec, nv)
                                        nsq = sc.on(
                                            "scalar",
                                            lambda eng, eq=eq,
                                            ts_i=ts_i: eng.activation(
                                                t_sq[:, :], t_y[:, 0:512],
                                                AF.Square,
                                                accum_out=t_ss4[
                                                    :, 4 * ts_i + eq:
                                                    4 * ts_i + eq + 1]),
                                            inc=act)
                                        nsq_last = nsq
                                        sc.wait("sync", vec, nv)
                                        dma(sc, "sync", st,
                                            y_scr[ts_i * 128:
                                                  (ts_i + 1) * 128,
                                                  eq * 512:(eq + 1) * 512],
                                            t_y[:, 0:512])
                                        sflag(sc)
                                        if eq == 3:
                                            row0 = tok0 + ts_i * 128
                                            y2p = t_y2
                                            xop = t_xo
                                            sc.wait("sync", vec, vec.n)
                                            dma(sc, "sync", ld, y2p,
                                                y_scr[ts_i * 128:
                                                      (ts_i + 1) * 128, :])
                                            dma(sc, "sync", ld, xop,
                                                xres_ap[row0:row0 + 128, :])
                                            fl_y = lflag(sc)
                                            sc.wait("vector", act, nsq)
                                            nv3 = sc.on(
                                                "vector",
                                                lambda eng, ts_i=ts_i:
                                                eng.reduce_sum(
                                                    t_sm8[:, ts_i:ts_i + 1],
                                                    t_ss4[:, 4 * ts_i:
                                                          4 * ts_i + 4],
                                                    axis=mybir.AxisListType.X),
                                                inc=vec)
                                            sc.wait("scalar", vec, nv3)
                                            na2 = sc.on(
                                                "scalar",
                                                lambda eng, ts_i=ts_i:
                                                eng.activation(
                                                    t_sd8[:, ts_i:ts_i + 1],
                                                    t_sm8[:, ts_i:ts_i + 1],
                                                    AF.Sqrt, scale=1.0 / D,
                                                    bias=RMS_EPS_FFN),
                                                inc=act)
                                            sc.wait("vector", act, na2)
                                            sc.on("vector",
                                                  lambda eng, ts_i=ts_i:
                                                  eng.reciprocal(
                                                      t_rs8[:, ts_i:ts_i + 1],
                                                      t_sd8[:, ts_i:
                                                            ts_i + 1]))
                                            sc.chain("vector", vec)
                                            sc.wait("vector", ldf, fl_y)
                                            if tail_stf:
                                                sc.wait("vector", stf,
                                                        tail_stf)
                                            nv2 = 0
                                            for cc2 in range(2):
                                                xcc = (t_xc1 if cc2 == 0
                                                       else t_xc2)
                                                sc.on(
                                                    "vector",
                                                    lambda eng, xcc=xcc,
                                                    y2p=y2p, cc2=cc2,
                                                    ts_i=ts_i:
                                                    eng.tensor_scalar(
                                                        xcc, y2p[
                                                            :, cc2 * 1024:
                                                            cc2 * 1024
                                                            + 1024],
                                                        t_rs8[:, ts_i:
                                                              ts_i + 1],
                                                        None, ALU.mult))
                                                nv2 = sc.on(
                                                    "vector",
                                                    lambda eng, xcc=xcc,
                                                    xop=xop, cc2=cc2:
                                                    eng.tensor_tensor(
                                                        xcc, xcc, xop[
                                                            :, cc2 * 1024:
                                                            cc2 * 1024
                                                            + 1024],
                                                        ALU.add), inc=vec)
                                                sc.wait("sync", vec, nv2)
                                                dma(sc, "sync", st,
                                                    xout_ap[
                                                        row0:row0 + 128,
                                                        cc2 * 1024:
                                                        cc2 * 1024 + 1024],
                                                    xcc)
                                            tail_stf = sflag(sc)
                                nmd_hist[2 * eq + seg] = nmd
                        carry["nmd6"] = nmd_hist[6]
                        carry["nmd7"] = nmd_hist[7]
                        carry["nvy_last"] = nvy[(3, 7)]
                    sc.wait("sync", st, st.n)

            sc = Script()
            start_next(sc)
            sc.wait("sync", col, 2)
            plan_ffn(sc, x1, x2)
            start_next = barrier(sc)
            phases.append(sc)
            check_done()

            sc = Script()
            start_next(sc)
            plan_ln(sc, lambda i: x2[i * 128:(i + 1) * 128, :], xln, 16)
            start_next = barrier(sc)
            phases.append(sc)
            check_done()

            sc = Script()
            start_next(sc)
            sc.wait("sync", cw, cw.n)
            plan_rope_prep(sc, "b")
            sc.wait("gpsimd", st, st.n)
            for src, dst in ((kb_own, ag_k), (vb_own, ag_v)):
                sc.on("gpsimd", lambda eng, s=src, d=dst: eng.collective_compute(
                    "AllGather", ALU.bypass,
                    replica_groups=[[0, 1], [2, 3], [4, 5], [6, 7]],
                    ins=[s.opt()], outs=[d.opt()]), inc=cc)
            sc.wait("gpsimd", cc, cc.n)
            start_next = barrier(sc)
            phases.append(sc)
            check_done()

            sc = Script()
            start_next(sc)
            sc.wait("sync", cw, cw.n)
            plan_attn(sc, "b")
            start_next = barrier(sc)
            phases.append(sc)
            check_done()

            sc = Script()
            start_next(sc)
            plan_ln(sc, lambda i: x3[i * 128:(i + 1) * 128, :], xln, 16)
            start_next = barrier(sc)
            phases.append(sc)
            check_done()

            sc = Script()
            start_next(sc)
            plan_ffn(sc, x3, out)
            sc.wait("sync", st, st.n)
            barrier(sc)
            phases.append(sc)
            check_done()

        except _StopPlanning:
            pass
        if tap_map:
            sc = Script()
            for e in Script.ENGINES:
                sc.wait(e, ph, ph.n)
            for t, (src_, dst) in tap_map.items():
                rows = src_.shape[0]
                for r0 in range(0, rows, 512):
                    r1 = min(rows, r0 + 512)
                    dma(sc, "sync", st, dst[r0:r1, :], src_[r0:r1, :])
            sc.wait("sync", st, st.n)
            phases.append(sc)
        for sc_i in phases:
            with nc.Block() as blk_:
                sc_i.run(blk_)

    return nc


def _rope_tables(pos, inv):
    ang = pos[:, None].astype(np.float64) * inv[None, :]
    return (np.cos(ang).astype(np.float32), np.sin(ang).astype(np.float32))


def _core_inputs(core, full):
    b, p = core // 2, core % 2
    blocks = [p + 2 * bi for bi in range(NB)]
    own_rows = np.concatenate([np.arange(g * 512, g * 512 + 512)
                               for g in blocks])
    x = np.asarray(full["L0_attn_a_x"])
    xb = x[b]
    x_own = np.ascontiguousarray(xb[own_rows])
    halos, kpos = [], []
    qpos = own_rows.astype(np.int64)
    for g in blocks:
        h0 = g * 512 - 512
        if h0 < 0:
            halos.append(np.zeros((512, D), np.float32))
            kpos.append(np.zeros(512, np.int64))
        else:
            halos.append(xb[h0:h0 + 512])
            kpos.append(np.arange(h0, h0 + 512))
        kpos.append(np.arange(g * 512, g * 512 + 512))
    x_halo = np.ascontiguousarray(np.concatenate(halos, axis=0))
    kpos = np.concatenate(kpos)

    cos_aq, sin_aq = _rope_tables(qpos, ROPE_A_INV)
    cos_ak, sin_ak = _rope_tables(kpos, ROPE_A_INV)
    cos_b_, sin_b_ = _rope_tables(qpos, ROPE_B_INV)

    mask_a = np.zeros((NB, 8, 128, 512), np.float32)
    for bi, g in enumerate(blocks):
        for j in range(8):
            kg = (g * 512 - 512 + 128 * j) + np.arange(128)[:, None]
            qg = g * 512 + np.arange(512)[None, :]
            mask_a[bi, j] = ((qg - kg >= 0) & (qg - kg < WIN) & (kg >= 0))
    mask_b = np.zeros((NB, 8, 128, 512), np.float32)
    for bi, g in enumerate(blocks):
        for m in range(8):
            kt = 8 * bi + m
            kg = kt * 128 + np.arange(128)[:, None]
            qg = g * 512 + np.arange(512)[None, :]
            mask_b[bi, m] = (kg <= qg)

    return dict(
        x_own=x_own, x_halo=x_halo,
        wq_a_t=np.ascontiguousarray(np.asarray(full["L0_attn_a_Wq"]).T),
        wk_a_t=np.ascontiguousarray(np.asarray(full["L0_attn_a_Wk"]).T),
        wv_a_t=np.ascontiguousarray(np.asarray(full["L0_attn_a_Wv"]).T),
        wo_a_t=np.ascontiguousarray(np.asarray(full["L0_attn_a_Wo"]).T),
        qgain=np.ascontiguousarray(
            np.asarray(full["L0_attn_a_q_gain"]).reshape(1, 16)),
        wq_b=np.ascontiguousarray(np.asarray(full["L0_attn_b_Wq"])),
        bq_b=np.ascontiguousarray(
            np.asarray(full["L0_attn_b_bq"]).reshape(1, D)),
        wk_b=np.ascontiguousarray(np.asarray(full["L0_attn_b_Wk"])),
        bk_b=np.ascontiguousarray(
            np.asarray(full["L0_attn_b_bk"]).reshape(1, 512)),
        wv_b=np.ascontiguousarray(np.asarray(full["L0_attn_b_Wv"])),
        bv_b=np.ascontiguousarray(
            np.asarray(full["L0_attn_b_bv"]).reshape(1, 512)),
        wo_b=np.ascontiguousarray(np.asarray(full["L0_attn_b_Wo"])),
        bo_b=np.ascontiguousarray(
            np.asarray(full["L0_attn_b_bo"]).reshape(1, D)),
        gup_pk=np.ascontiguousarray(
            np.asarray(full["L0_ffn_gate_up_packed"])[
                core * (2 * I_FF // 8):(core + 1) * (2 * I_FF // 8)]),
        gup_sct=np.ascontiguousarray(
            np.asarray(full["L0_ffn_gate_up_scales"]).T[
                core * (2 * I_FF // 8):(core + 1) * (2 * I_FF // 8)]),
        dn_pk=np.ascontiguousarray(
            np.asarray(full["L0_ffn_down_packed"])[
                core * (D // 8):(core + 1) * (D // 8)]),
        dn_sct=np.ascontiguousarray(
            np.asarray(full["L0_ffn_down_scales"]).T[
                core * (D // 8):(core + 1) * (D // 8)]),
        cos_aq=cos_aq, sin_aq=sin_aq, cos_ak=cos_ak, sin_ak=sin_ak,
        cos_b=cos_b_, sin_b=sin_b_,
        mask_a=mask_a.astype(BF), mask_b=mask_b.astype(BF),
    )


_NC_CACHE = {}


def kernel(**inputs):
    taps = tuple(DEBUG_TAPS)
    key = (taps, PHASE_LIMIT)
    if key not in _NC_CACHE:
        _NC_CACHE[key] = build_nc(taps, PHASE_LIMIT)
    nc = _NC_CACHE[key]
    in_maps = [_core_inputs(c, inputs) for c in range(NCORES)]
    res = run_bass_kernel_spmd(nc, in_maps, core_ids=list(range(NCORES)))
    outp = np.zeros((B, S, D), np.float32)
    for c in range(NCORES):
        b, p = c // 2, c % 2
        o = np.asarray(res.results[c]["out"])
        for bi in range(NB):
            g = p + 2 * bi
            outp[b, g * 512:(g + 1) * 512, :] = o[bi * 512:(bi + 1) * 512, :]
    if taps:
        kernel.tap_results = res.results
    return outp


# version 20
# speedup vs baseline: 1.0821x; 1.0821x over previous
import contextlib
import math
import numpy as np
import ml_dtypes

import concourse.bass as bass
import concourse.mybir as mybir
from concourse.ap import AP
from concourse.bass_utils import run_bass_kernel_spmd

F32 = mybir.dt.float32
F16 = mybir.dt.float16
BF16 = mybir.dt.bfloat16
I32 = mybir.dt.int32
U8 = mybir.dt.uint8
U16 = mybir.dt.uint16
AF = mybir.ActivationFunctionType
ALU = mybir.AluOpType

BF = ml_dtypes.bfloat16

B, S, D = 4, 4096, 2048
NCORES = 8
TPC = 2048
NB = 4
H, KVH, HD = 16, 4, 128
GRP = H // KVH
WIN = 512
I_FF = 5632
LN_EPS = 1e-5
RMS_EPS_FFN = 1e-6
RMS_EPS_QK = 1.1920928955078125e-07
SHIFT_B = 40.0
ROPE_A_INV = (10000.0 ** (-(np.arange(64, dtype=np.float64) / 64.0)))
ROPE_B_INV = (10000.0 ** (-(np.arange(0, 2048, 2, dtype=np.float64)[:64]
                            / 2048.0)))

DEBUG_TAPS = ()
PHASE_LIMIT = None
QKV_FLAGS = dict(nblk=4, do_q=True, do_kv=True, do_rope=True)


class Sem:
    def __init__(self, handle):
        self.h = handle
        self.n = 0

    def bump(self, k):
        self.n += k
        return self.n


class Script:
    ENGINES = ("sync", "tensor", "vector", "scalar", "gpsimd")

    def __init__(self):
        self.items = {e: [] for e in self.ENGINES}

    def on(self, engine, fn, inc=None, inc_k=1):
        if inc is None:
            self.items[engine].append(fn)
            return None
        n = inc.bump(inc_k)
        self.items[engine].append(
            lambda eng, f=fn, s=inc.h, k=inc_k: f(eng).then_inc(s, k))
        return n

    def chain(self, engine, sem):
        n = self.inc_last(engine, sem)
        self.wait(engine, sem, n)

    def inc_last(self, engine, sem, k=1):
        n = sem.bump(k)
        prev = self.items[engine][-1]
        self.items[engine][-1] = (
            lambda eng, f=prev, s=sem.h, kk=k: f(eng).then_inc(s, kk))
        return n

    def wait(self, engine, sem, val=None):
        v = sem.n if val is None else val
        if v > 0:
            self.items[engine].append(
                lambda eng, s=sem.h, v=v: eng.wait_ge(s, v))

    def run(self, block):
        for name in self.ENGINES:
            fns = self.items[name]
            getattr(block, name)(lambda eng, fns=fns: [f(eng) for f in fns])


class _StopPlanning(Exception):
    pass


def build_nc(taps=(), phase_limit=None):
    nc = bass.Bass("TRN2", target_bir_lowering=False,
                   detect_race_conditions=False)
    _uid = [0]

    def U(s):
        _uid[0] += 1
        return f"{s}_{_uid[0]}"

    def inp(name, shape, dt):
        return nc.dram_tensor(name, shape, dt, kind="ExternalInput").ap()

    def intl(name, shape, dt, shared=False):
        if shared:
            return nc.dram_tensor(name, shape, dt, kind="Internal",
                                  addr_space="Shared").ap()
        return nc.dram_tensor(name, shape, dt, kind="Internal").ap()

    x_own = inp("x_own", [TPC, D], F32)
    x_halo = inp("x_halo", [NB * 512, D], F32)
    wq_a_t = inp("wq_a_t", [D, D], F32)
    wk_a_t = inp("wk_a_t", [D, 512], F32)
    wv_a_t = inp("wv_a_t", [D, 512], F32)
    wo_a_t = inp("wo_a_t", [D, D], F32)
    qgain = inp("qgain", [1, 16], F32)
    wq_b = inp("wq_b", [D, D], F32)
    bq_b = inp("bq_b", [1, D], F32)
    wk_b = inp("wk_b", [D, 512], F32)
    bk_b = inp("bk_b", [1, 512], F32)
    wv_b = inp("wv_b", [D, 512], F32)
    bv_b = inp("bv_b", [1, 512], F32)
    wo_b = inp("wo_b", [D, D], F32)
    bo_b = inp("bo_b", [1, D], F32)
    gup_pk = inp("gup_pk", [2 * I_FF // 8, 256], I32)
    gup_sct = inp("gup_sct", [2 * I_FF // 8, 16], F16)
    dn_pk = inp("dn_pk", [D // 8, 704], I32)
    dn_sct = inp("dn_sct", [D // 8, 44], F16)
    cos_aq = inp("cos_aq", [TPC, 64], F32)
    sin_aq = inp("sin_aq", [TPC, 64], F32)
    cos_ak = inp("cos_ak", [NB * 1024, 64], F32)
    sin_ak = inp("sin_ak", [NB * 1024, 64], F32)
    cos_b = inp("cos_b", [TPC, 64], F32)
    sin_b = inp("sin_b", [TPC, 64], F32)
    mask_a = inp("mask_a", [NB, 8, 128, 512], BF16)
    mask_b = inp("mask_b", [NB, 8, 128, 512], BF16)

    out = nc.dram_tensor("out", [TPC, D], F32, kind="ExternalOutput").ap()

    x1 = intl("x1", [TPC, D], F32)
    x2 = intl("x2", [TPC, D], F32)
    x3 = intl("x3", [TPC, D], F32)
    xln = intl("xln", [TPC, D], BF16)
    xln_a = intl("xln_a", [NB * 1024, D], BF16)
    qa = intl("qa", [TPC, D], BF16)
    ka_buf = intl("ka_buf", [NB * 1024, 512], BF16)
    va_buf = intl("va_buf", [NB * 1024, 512], BF16)
    qb = intl("qb", [TPC, D], BF16)
    kb_own = intl("kb_own", [TPC, 512], BF16)
    vb_own = intl("vb_own", [TPC, 512], BF16)
    ag_k = intl("ag_k", [2 * TPC, 512], BF16)
    ag_v = intl("ag_v", [2 * TPC, 512], BF16)
    w_gu = intl("w_gu", [2 * I_FF, D], BF16)
    w_dn = intl("w_dn", [D, I_FF], BF16)
    w_gu_sh = intl("w_gu_sh", [2 * I_FF // 8, D], BF16)
    w_dn_sh = intl("w_dn_sh", [D // 8, I_FF], BF16)
    y_scr = intl("y_scr", [1024, D], F32)
    wbf = {}
    for name, src_ap, shape in (
            ("wq_a_t", wq_a_t, [D, D]), ("wk_a_t", wk_a_t, [D, 512]),
            ("wv_a_t", wv_a_t, [D, 512]), ("wo_a_t", wo_a_t, [D, D]),
            ("wq_b", wq_b, [D, D]), ("wk_b", wk_b, [D, 512]),
            ("wv_b", wv_b, [D, 512]), ("wo_b", wo_b, [D, D])):
        wbf[name] = (src_ap, src_ap)

    tap_map = {}
    if taps:
        tapable = dict(x1=x1, x2=x2, x3=x3, xln_a=xln_a, xln=xln, qa=qa,
                       ka=ka_buf, va=va_buf, qb=qb, kb=kb_own, vb=vb_own,
                       agk=ag_k, agv=ag_v, wgu=w_gu, wdn=w_dn)
        for t in taps:
            src = tapable[t]
            tap_map[t] = (src, nc.dram_tensor(
                "tap_" + t, list(src.shape), src.dtype,
                kind="ExternalOutput").ap())

    ones_bf_d = nc.inline_tensor(np.ones((128, 128), dtype=BF),
                                 name="ones_bf").ap()
    ones_f32_d = nc.inline_tensor(np.ones((1, 128), np.float32),
                                  name="ones_f32").ap()

    with contextlib.ExitStack() as top:
        ec = top.enter_context
        sb_ones = ec(nc.sbuf_tensor(U("sb_ones"), [128, 128], BF16))
        sb_ones_f32 = ec(nc.sbuf_tensor(U("sb_ones_f32"), [1, 128], F32))
        sb_c7 = ec(nc.sbuf_tensor(U("sb_c7"), [128, 1], I32))
        sb_c8 = ec(nc.sbuf_tensor(U("sb_c8"), [128, 1], I32))
        sb_gsc = ec(nc.sbuf_tensor(U("sb_gsc"), [128, 16], F32))
        sb_gbias = ec(nc.sbuf_tensor(U("sb_gbias"), [128, 16], F32))
        CONSTS = [LN_EPS, 1.0 / 128.0, RMS_EPS_QK, 1.0 / D, RMS_EPS_FFN,
                  -SHIFT_B]
        sb_const = ec(nc.sbuf_tensor(U("sb_const"), [128, len(CONSTS)], F32))
        for ci, cv in enumerate(CONSTS):
            nc.const_aps.register(F32, cv, sb_const[:, ci:ci + 1])

        sems = {k: Sem(ec(nc.semaphore(k)))
                for k in ("ld", "st", "ldf", "stf", "mm", "act", "vec",
                          "cc", "ccf", "col", "pl", "ph", "cw")}
        ld, st, ldf, stf, mm_s, act, vec, cc, ccf, col, pl, ph, cw = (
            sems[k] for k in ("ld", "st", "ldf", "stf", "mm", "act", "vec",
                              "cc", "ccf", "col", "pl", "ph", "cw"))

        def lflag(sc):
            sc.wait("sync", ld, ld.n)
            return sc.on("sync", lambda eng: eng.nop(), inc=ldf)

        def sflag(sc):
            sc.wait("sync", st, st.n)
            return sc.on("sync", lambda eng: eng.nop(), inc=stf)

        def cflag(sc):
            sc.wait("gpsimd", cc, cc.n)
            return sc.on("gpsimd", lambda eng: eng.nop(), inc=ccf)

        def fence(sc):
            fl = lflag(sc)
            fs = sflag(sc)
            sc.wait("gpsimd", cc, cc.n)
            gp = sc.on("gpsimd", lambda eng: eng.nop(), inc=ccf)
            for e in Script.ENGINES:
                sc.wait(e, ldf, fl)
                sc.wait(e, stf, fs)
                sc.wait(e, ccf, gp)
                for s in (mm_s, act, vec, pl):
                    sc.wait(e, s)

        def barrier(sc):
            fence(sc)
            for e in Script.ENGINES:
                ph.bump(1)
                sc.items[e].append(
                    lambda eng, s=ph.h: eng.nop().then_inc(s, 1))
            target = ph.n

            def start_waits(next_sc):
                for e in Script.ENGINES:
                    next_sc.items[e].append(
                        lambda eng, s=ph.h, v=target: eng.wait_ge(s, v))
            return start_waits

        def dma(sc, engine, sem, dst, src, transpose=False):
            def f(eng, d=dst, s=src, t=transpose):
                return (eng.dma_start(d, s, transpose=True) if t
                        else eng.dma_start(d, s))
            return sc.on(engine, f, inc=sem, inc_k=16)

        phases = []

        def check_done():
            if phase_limit is not None and len(phases) >= phase_limit:
                raise _StopPlanning()

        sc = Script()
        dma(sc, "sync", ld, sb_ones[:, :], ones_bf_d[:, :])
        n0 = dma(sc, "sync", ld, sb_ones_f32[:, :], ones_f32_d[:, :])
        sc.on("vector", lambda eng: eng.memset(sb_c7[:, :], 7))
        sc.on("vector", lambda eng: eng.memset(sb_c8[:, :], 8), inc=vec)
        for ci, cv in enumerate(CONSTS):
            sc.on("vector", lambda eng, ci=ci, cv=cv: eng.memset(
                sb_const[:, ci:ci + 1], cv))
        sc.on("vector", lambda eng: eng.nop(), inc=vec)


        with contextlib.ExitStack() as ps0:
            pec = ps0.enter_context
            sb_g = pec(nc.sbuf_tensor(U("p0_g"), [1, 16], F32))
            sb_gt = pec(nc.sbuf_tensor(U("p0_gt"), [1, 16], F32))
            ps_g = pec(nc.psum_tensor(U("p0_psg"), [128, 16], F32))
            dma(sc, "sync", ld, sb_g[:, :], qgain[:, :])
            fg = lflag(sc)
            sc.wait("scalar", ldf, fg)
            na = sc.on("scalar", lambda eng: eng.activation(
                sb_gt[:, :], sb_g[:, :], AF.Abs), inc=act)
            sc.wait("vector", act, na)
            sc.on("vector", lambda eng: eng.tensor_scalar(
                sb_gt[:, :], sb_gt[:, :], math.sqrt(128.0), 40.0,
                ALU.mult, ALU.min))
            sc.chain("vector", vec)
            sc.on("vector", lambda eng: eng.tensor_scalar(
                sb_gt[:, :], sb_gt[:, :], -1.0, None, ALU.mult))
            sc.chain("vector", vec)
            nv = sc.on("vector", lambda eng: eng.tensor_scalar(
                sb_g[:, :], sb_g[:, :], 1.0 / math.sqrt(128.0), None,
                ALU.mult), inc=vec)
            sc.wait("tensor", vec, nv)
            sc.wait("tensor", ldf, fg)
            nm = sc.on("tensor", lambda eng: eng.matmul(
                ps_g[:, :], sb_ones_f32[:, :], sb_gt[:, :],
                start=True, stop=True), inc=mm_s)
            sc.wait("vector", mm_s, nm)
            nv = sc.on("vector", lambda eng: eng.tensor_copy(
                sb_gbias[:, :], ps_g[:, :]), inc=vec)
            sc.wait("tensor", vec, nv)
            nm = sc.on("tensor", lambda eng: eng.matmul(
                ps_g[:, :], sb_ones_f32[:, :], sb_g[:, :],
                start=True, stop=True), inc=mm_s)
            sc.wait("vector", mm_s, nm)
            sc.on("vector", lambda eng: eng.tensor_copy(
                sb_gsc[:, :], ps_g[:, :]), inc=vec)

            def plan_dequant(pk_ap, sct_ap, w_ap, nrows, words):
                nib = 4 * words
                groups = (8 * words) // 128
                with contextlib.ExitStack() as ds:
                    dec = ds.enter_context
                    t_pk = dec(nc.sbuf_tensor(U("dq_pk"), [128, words], I32))
                    t_lo = dec(nc.sbuf_tensor(U("dq_lo"), [128, words], I32))
                    t_hi = dec(nc.sbuf_tensor(U("dq_hi"), [128, words], I32))
                    t_e = dec(nc.sbuf_tensor(U("dq_e"), [128, nib], U16))
                    t_n7 = dec(nc.sbuf_tensor(U("dq_n7"), [128, nib], U16))
                    t_t1 = dec(nc.sbuf_tensor(U("dq_t1"), [128, nib], U16))
                    t_t3 = dec(nc.sbuf_tensor(U("dq_t3"), [128, nib], U16))
                    t_sg = dec(nc.sbuf_tensor(U("dq_sg"), [128, nib], U16))
                    t_lh = dec(nc.sbuf_tensor(U("dq_lh"), [128, 2 * nib], U16))
                    t_w = dec(nc.sbuf_tensor(U("dq_w"), [128, 2 * nib], U16))
                    t_s16 = dec(nc.sbuf_tensor(U("dq_s16"), [128, groups], F16))
                    t_sc = dec(nc.sbuf_tensor(U("dq_sc"), [128, groups], F32))
                    t_wo = dec(nc.sbuf_tensor(U("dq_wo"), [128, 2 * nib], BF16))
                    il_src = AP(t_lh[:, :].tensor, t_lh[:, :].offset,
                                [list(t_lh[:, :].ap[0]), [1, nib], [nib, 2]])
                    prev_store = [0]
                    for r in range(nrows // 128):
                        r0 = r * 128
                        sc.wait("vector", stf, prev_store[0])
                        dma(sc, "sync", ld, t_pk[:, :],
                            pk_ap[r0:r0 + 128, :])
                        dma(sc, "sync", ld, t_s16[:, :],
                            sct_ap[r0:r0 + 128, :])
                        fl = lflag(sc)
                        sc.wait("vector", ldf, fl)
                        sc.on("vector", lambda eng: eng.tensor_copy(
                            t_sc[:, :], t_s16[:, :]))
                        sc.on("vector", lambda eng: eng.tensor_scalar(
                            t_lo[:, :], t_pk[:, :], 0x0F0F0F0F, None,
                            ALU.bitwise_and))
                        sc.on("vector", lambda eng: eng.tensor_scalar(
                            t_hi[:, :], t_pk[:, :], 4, 0x0F0F0F0F,
                            ALU.logical_shift_right, ALU.bitwise_and))
                        for plane, tsrc in ((0, t_lo), (1, t_hi)):
                            lhs = t_lh[:, plane * nib:(plane + 1) * nib]
                            sc.on("vector", lambda eng, s=tsrc:
                                  eng.tensor_copy(t_e[:, :],
                                                  s[:, :].bitcast(U8)))
                            sc.on("vector", lambda eng: eng.tensor_scalar(
                                t_n7[:, :], t_e[:, :], sb_c7[:, :], None,
                                ALU.bitwise_and))
                            sc.on("vector", lambda eng: eng.tensor_scalar(
                                t_t1[:, :], t_n7[:, :], 64, 16128,
                                ALU.mult, ALU.add))
                            sc.on("vector", lambda eng: eng.tensor_scalar(
                                t_t3[:, :], t_n7[:, :], 4, 16128,
                                ALU.min, ALU.mult))
                            sc.on("vector", lambda eng: eng.tensor_tensor(
                                t_t1[:, :], t_t1[:, :], t_t3[:, :], ALU.min))
                            sc.on("vector", lambda eng: eng.tensor_scalar(
                                t_sg[:, :], t_e[:, :], sb_c8[:, :], None,
                                ALU.bitwise_and))
                            sc.on("vector", lambda eng: eng.tensor_scalar(
                                t_sg[:, :], t_sg[:, :], 4096, None, ALU.mult))
                            sc.on("vector", lambda eng, o=lhs:
                                  eng.tensor_tensor(o, t_t1[:, :], t_sg[:, :],
                                                    ALU.add))
                        sc.on("vector", lambda eng: eng.tensor_copy(
                            t_w[:, :], il_src))
                        wv_ = t_w[:, :].bitcast(BF16)
                        for g in range(groups):
                            last = sc.on(
                                "vector",
                                lambda eng, g=g: eng.tensor_scalar(
                                    t_wo[:, g * 128:(g + 1) * 128],
                                    wv_[:, g * 128:(g + 1) * 128],
                                    t_sc[:, g:g + 1], None, ALU.mult),
                                inc=(vec if g == groups - 1 else None))
                        sc.wait("sync", vec, last)
                        dma(sc, "sync", st, w_ap[r0:r0 + 128, :], t_wo[:, :])
                        prev_store[0] = sflag(sc)

            plan_dequant(gup_pk, gup_sct, w_gu_sh, 2 * I_FF // 8, 256)
            plan_dequant(dn_pk, dn_sct, w_dn_sh, D // 8, 704)

            start_next = barrier(sc)
            phases.append(sc)
            check_done()

        def plan_ln(sc, srcs, dst_ap, n_tiles):
            with contextlib.ExitStack() as ls:
                lec = ls.enter_context
                t_x = [lec(nc.sbuf_tensor(U(f"ln_x{i}"), [128, D], F32))
                       for i in range(2)]
                t_o = [lec(nc.sbuf_tensor(U(f"ln_o{i}"), [128, D], BF16))
                       for i in range(2)]
                t_sq = lec(nc.sbuf_tensor(U("ln_sq"), [128, D], BF16))
                t_ss = lec(nc.sbuf_tensor(U("ln_ss"), [128, 1], F32))
                t_s = lec(nc.sbuf_tensor(U("ln_s"), [128, 1], F32))
                t_m = lec(nc.sbuf_tensor(U("ln_m"), [128, 1], F32))
                t_v = lec(nc.sbuf_tensor(U("ln_v"), [128, 1], F32))
                t_sd = lec(nc.sbuf_tensor(U("ln_sd"), [128, 1], F32))
                t_r = lec(nc.sbuf_tensor(U("ln_r"), [128, 1], F32))
                hist = [0, 0]
                for i in range(n_tiles):
                    xt = t_x[i % 2]
                    ot = t_o[i % 2]
                    sc.wait("sync", vec, hist[i % 2])
                    dma(sc, "sync", ld, xt[:, :], srcs(i))
                    fl = lflag(sc)
                    sc.wait("scalar", vec, vec.n)
                    sc.wait("scalar", ldf, fl)
                    na = sc.on("scalar", lambda eng, x=xt: eng.activation(
                        t_sq[:, :], x[:, :], AF.Square,
                        accum_out=t_ss[:, :]), inc=act)
                    sc.wait("vector", ldf, fl)
                    sc.on("vector", lambda eng, x=xt: eng.reduce_sum(
                        t_s[:, :], x[:, :], axis=mybir.AxisListType.X))
                    sc.chain("vector", vec)
                    sc.wait("vector", act, na)
                    sc.on("vector", lambda eng: eng.tensor_scalar(
                        t_m[:, :], t_s[:, :], 1.0 / D, None, ALU.mult))
                    sc.chain("vector", vec)
                    sc.on("vector", lambda eng: eng.tensor_tensor(
                        t_v[:, :], t_m[:, :], t_m[:, :], ALU.mult))
                    sc.chain("vector", vec)
                    sc.on("vector", lambda eng: eng.tensor_scalar(
                        t_s[:, :], t_ss[:, :], 1.0 / D, None, ALU.mult))
                    sc.chain("vector", vec)
                    nv = sc.on("vector", lambda eng: eng.tensor_tensor(
                        t_v[:, :], t_s[:, :], t_v[:, :], ALU.subtract),
                        inc=vec)
                    sc.wait("scalar", vec, nv)
                    na2 = sc.on("scalar", lambda eng: eng.activation(
                        t_sd[:, :], t_v[:, :], AF.Sqrt, bias=LN_EPS), inc=act)
                    sc.wait("vector", act, na2)
                    sc.on("vector", lambda eng: eng.reciprocal(
                        t_r[:, :], t_sd[:, :]))
                    sc.chain("vector", vec)
                    nv2 = sc.on("vector", lambda eng, x=xt, o=ot:
                                eng.tensor_scalar(
                                    o[:, :], x[:, :], t_m[:, :], t_r[:, :],
                                    ALU.subtract, ALU.mult), inc=vec)
                    hist[i % 2] = nv2
                    sc.wait("sync", vec, nv2)
                    dma(sc, "sync", st, dst_ap[i * 128:(i + 1) * 128, :],
                        ot[:, :])
                sc.wait("sync", st, st.n)

        try:
            sc = Script()
            start_next(sc)

            def ln_a_src(i):
                blk, sub = divmod(i, 8)
                if sub < 4:
                    r = blk * 512 + sub * 128
                    return x_halo[r:r + 128, :]
                r = blk * 512 + (sub - 4) * 128
                return x_own[r:r + 128, :]
            plan_ln(sc, ln_a_src, xln_a, NB * 8)
            start_next = barrier(sc)
            phases.append(sc)
            check_done()

            def plan_rope_prep(sc, variant):
                FL = QKV_FLAGS
                rows_blk = 1024 if variant == "a" else 512
                tiles_per_blk = 8 if variant == "a" else 4
                src_ln = xln_a if variant == "a" else xln
                with contextlib.ExitStack() as qs:
                    qec = qs.enter_context
                    if variant == "a":
                        wq_d, wk_d, wv_d = (wbf["wq_a_t"][1], wbf["wk_a_t"][1],
                                            wbf["wv_a_t"][1])
                    else:
                        wq_d, wk_d, wv_d = (wbf["wq_b"][1], wbf["wk_b"][1],
                                            wbf["wv_b"][1])
                    t_w = qec(nc.sbuf_tensor(U("qp_w"), [128, 16 * 3072], BF16))

                    def wq_t(k):
                        return t_w[:, k * 3072:k * 3072 + 2048]

                    def wk_t(k):
                        return t_w[:, k * 3072 + 2048:k * 3072 + 2560]

                    def wv_t(k):
                        return t_w[:, k * 3072 + 2560:k * 3072 + 3072]
                    for k in range(16):
                        dma(sc, "gpsimd", cc, wk_t(k),
                            wk_d[k * 128:(k + 1) * 128, :])
                        dma(sc, "gpsimd", cc, wv_t(k),
                            wv_d[k * 128:(k + 1) * 128, :])
                    fw_kv = cflag(sc)
                    for k in range(16):
                        dma(sc, "gpsimd", cc, wq_t(k),
                            wq_d[k * 128:(k + 1) * 128, :])
                    fw_q = cflag(sc)
                    ps_q2 = [qec(nc.psum_tensor(U("qp_psq"), [128, 1024],
                                                F32)) for _ in range(2)]
                    ps_kv2 = [qec(nc.psum_tensor(U("qp_pskv"), [128, 1024],
                                                 F32)) for _ in range(2)]
                    if variant == "b":
                        t_bq = qec(nc.sbuf_tensor(U("qp_bq"), [128, 2048], F32))
                        t_bkv = qec(nc.sbuf_tensor(U("qp_bkv"), [128, 1024], F32))
                        t_br = qec(nc.sbuf_tensor(U("qp_br"), [1, 2048], F32))
                        t_br2 = qec(nc.sbuf_tensor(U("qp_br2"), [1, 1024], F32))
                        ps_b = ps_q2[0]
                        dma(sc, "sync", ld, t_br[:, :], bq_b[:, :])
                        dma(sc, "sync", ld, t_br2[:, 0:512], bk_b[:, :])
                        dma(sc, "sync", ld, t_br2[:, 512:1024], bv_b[:, :])
                        flb = lflag(sc)
                        sc.wait("tensor", ldf, flb)
                        for j, (dst, row) in enumerate(
                                [(t_bq[:, jj * 512:(jj + 1) * 512],
                                  t_br[:, jj * 512:(jj + 1) * 512])
                                 for jj in range(4)] +
                                [(t_bkv[:, jj * 512:(jj + 1) * 512],
                                  t_br2[:, jj * 512:(jj + 1) * 512])
                                 for jj in range(2)]):
                            nm = sc.on("tensor", lambda eng, r=row: eng.matmul(
                                ps_b[:, 0:512], sb_ones_f32[:, :], r,
                                start=True, stop=True), inc=mm_s)
                            sc.wait("vector", mm_s, nm)
                            nv = sc.on("vector", lambda eng, d=dst:
                                       eng.tensor_copy(d, ps_b[:, 0:512]),
                                       inc=vec)
                            sc.wait("tensor", vec, nv)

                    t_xT = [qec(nc.sbuf_tensor(U(f"qp_xT{i}"),
                                               [128, 16 * rows_blk], BF16))
                            for i in range(2)]
                    t_cq = qec(nc.sbuf_tensor(U("qp_cq"), [128, 64], F32))
                    t_sq_ = qec(nc.sbuf_tensor(U("qp_sq"), [128, 64], F32))
                    t_ck = qec(nc.sbuf_tensor(U("qp_ck"), [128, 64], F32))
                    t_sk = qec(nc.sbuf_tensor(U("qp_sk"), [128, 64], F32))
                    t_qf = qec(nc.sbuf_tensor(U("qp_qf"), [128, 2048], F32))
                    t_kf = qec(nc.sbuf_tensor(U("qp_kf"), [128, 1024], F32))
                    t_rr = qec(nc.sbuf_tensor(U("qp_rr"), [128, 2048], F32))
                    t_rk = qec(nc.sbuf_tensor(U("qp_rk"), [128, 512], F32))
                    t_sqs = qec(nc.sbuf_tensor(U("qp_sqs"), [128, 2048], F32))
                    t_ss = qec(nc.sbuf_tensor(U("qp_ss"), [128, 20], F32))
                    t_sd = qec(nc.sbuf_tensor(U("qp_sd"), [128, 20], F32))
                    t_rst = qec(nc.sbuf_tensor(U("qp_rst"), [128, 20], F32))
                    t_obf = qec(nc.sbuf_tensor(U("qp_obf"), [128, 2048], BF16))
                    t_kvbf = qec(nc.sbuf_tensor(U("qp_kvbf"), [128, 1024], BF16))

                    xT_guard = [0, 0]
                    kv_guard = [0, 0]
                    q_guard = [0, 0]
                    sc.wait("tensor", ccf, fw_kv)
                    q_w_waited = [False]
                    nblk_q = min(NB, FL["nblk"])

                    def load_xT(b2):
                        xTb = t_xT[b2 % 2]
                        for k in range(16):
                            dma(sc, "sync", ld,
                                xTb[:, k * rows_blk:(k + 1) * rows_blk],
                                src_ln[b2 * rows_blk:(b2 + 1) * rows_blk,
                                       k * 128:(k + 1) * 128],
                                transpose=True)

                    for blk in range(nblk_q):
                        xT = t_xT[blk % 2]
                        if blk == 0:
                            load_xT(0)
                        fl_xT = lflag(sc)
                        if blk + 1 < nblk_q:
                            sc.wait("sync", mm_s, xT_guard[(blk + 1) % 2])
                            load_xT(blk + 1)
                        for sub in range(tiles_per_blk):
                            row0 = blk * rows_blk + sub * 128
                            is_own = (variant == "b") or (sub >= 4)
                            own_row0 = (blk * 512 + (sub - 4) * 128
                                        if variant == "a" else row0)
                            ps_kv = ps_kv2[sub % 2]
                            sc.wait("tensor", ldf, fl_xT)
                            if kv_guard[sub % 2]:
                                sc.wait("tensor", vec, kv_guard[sub % 2])
                            for k in range(16):
                                for half in range(2):
                                    nmkv = sc.on("tensor", lambda eng, k=k,
                                                 h=half, xT=xT, sub=sub,
                                                 ps_kv=ps_kv:
                                                 eng.matmul(
                                                     ps_kv[:, h * 512:
                                                           (h + 1) * 512],
                                                     xT[:, k * rows_blk +
                                                        sub * 128:
                                                        k * rows_blk +
                                                        sub * 128 + 128],
                                                     (wk_t(k) if h == 0
                                                      else wv_t(k)),
                                                     start=(k == 0),
                                                     stop=(k == 15)),
                                                 inc=(mm_s if k == 15 and
                                                      half == 1 else None))
                            nmq = [None, None]
                            if is_own:
                                if not q_w_waited[0]:
                                    sc.wait("tensor", ccf, fw_q)
                                    q_w_waited[0] = True
                                for qc in range(2):
                                    psq = ps_q2[qc]
                                    if q_guard[qc]:
                                        sc.wait("tensor", vec, q_guard[qc])
                                    for k in range(16):
                                        for j in range(2):
                                            nmq[qc] = sc.on(
                                                "tensor", lambda eng, k=k, j=j,
                                                qc=qc, xT=xT, sub=sub, psq=psq:
                                                eng.matmul(
                                                    psq[:, j * 512:
                                                        (j + 1) * 512],
                                                    xT[:, k * rows_blk +
                                                       sub * 128:
                                                       k * rows_blk + sub * 128
                                                       + 128],
                                                    wq_t(k)[:, qc * 1024 +
                                                            j * 512:
                                                            qc * 1024 + j * 512
                                                            + 512],
                                                    start=(k == 0),
                                                    stop=(k == 15)),
                                                inc=(mm_s if k == 15 and j == 1
                                                     else None))

                            if variant == "a":
                                ck0 = cos_ak[row0:row0 + 128, :]
                                sk0 = sin_ak[row0:row0 + 128, :]
                            else:
                                ck0 = cos_b[row0:row0 + 128, :]
                                sk0 = sin_b[row0:row0 + 128, :]
                            sc.wait("sync", vec, vec.n)
                            dma(sc, "sync", ld, t_ck[:, :], ck0)
                            dma(sc, "sync", ld, t_sk[:, :], sk0)
                            if is_own and variant == "a":
                                dma(sc, "sync", ld, t_cq[:, :],
                                    cos_aq[own_row0:own_row0 + 128, :])
                                dma(sc, "sync", ld, t_sq_[:, :],
                                    sin_aq[own_row0:own_row0 + 128, :])
                            fl_rope = lflag(sc)

                            sc.wait("vector", mm_s, nmkv)
                            sc.wait("vector", ldf, fl_rope)
                            nst = st.n
                            if not FL["do_kv"]:
                                pass
                            elif variant == "a":
                                sc.on("vector", lambda eng, ps_kv=ps_kv:
                                      eng.tensor_copy(
                                          t_kf[:, 0:512], ps_kv[:, 0:512]))
                                nvk = sc.on("vector", lambda eng, ps_kv=ps_kv:
                                            eng.tensor_copy(
                                                t_kf[:, 512:1024],
                                                ps_kv[:, 512:1024]), inc=vec)
                                sc.wait("scalar", vec, nvk)
                                for hh in range(4):
                                    sc.on("scalar", lambda eng, hh=hh:
                                          eng.activation(
                                              t_sqs[:, hh * 128:(hh + 1) * 128],
                                              t_kf[:, hh * 128:(hh + 1) * 128],
                                              AF.Square,
                                              accum_out=t_ss[:, 16 + hh:17 + hh]))
                                sc.chain("scalar", act)
                                nak = sc.on("scalar", lambda eng: eng.activation(
                                    t_sd[:, 16:20], t_ss[:, 16:20], AF.Sqrt,
                                    scale=1.0 / 128.0, bias=RMS_EPS_QK), inc=act)
                                sc.wait("vector", act, nak)
                                sc.on("vector", lambda eng: eng.reciprocal(
                                    t_rst[:, 16:20], t_sd[:, 16:20]))
                                sc.chain("vector", vec)
                                for hh in range(4):
                                    b0 = hh * 128
                                    x1a = t_kf[:, b0:b0 + 64]
                                    x2a = t_kf[:, b0 + 64:b0 + 128]
                                    o1 = t_rk[:, b0:b0 + 64]
                                    o2 = t_rk[:, b0 + 64:b0 + 128]
                                    sc.on("vector", lambda eng, a=x1a, o=o1:
                                          eng.tensor_tensor(o, a, t_ck[:, :],
                                                            ALU.mult))
                                    sc.on("vector", lambda eng, a=x2a, o=o2:
                                          eng.tensor_tensor(o, a, t_sk[:, :],
                                                            ALU.mult))
                                    sc.on("vector", lambda eng, o=o1, o2=o2:
                                          eng.tensor_tensor(o, o, o2, ALU.add))
                                    sc.on("vector", lambda eng, a=x2a, o=o2:
                                          eng.tensor_tensor(o, a, t_ck[:, :],
                                                            ALU.mult))
                                    sc.on("vector", lambda eng, a=x1a, s=x2a:
                                          eng.tensor_tensor(s, a, t_sk[:, :],
                                                            ALU.mult))
                                    sc.on("vector", lambda eng, o=o2, s=x2a:
                                          eng.tensor_tensor(o, o, s,
                                                            ALU.subtract))
                                    sc.on("vector", lambda eng, hh=hh, b0=b0:
                                          eng.tensor_scalar(
                                              t_kvbf[:, b0:b0 + 128],
                                              t_rk[:, b0:b0 + 128],
                                              t_rst[:, 16 + hh:17 + hh], None,
                                              ALU.mult))
                                nvv = sc.on("vector", lambda eng,
                                            ps_kv=ps_kv: eng.tensor_copy(
                                                t_kvbf[:, 512:1024],
                                                ps_kv[:, 512:1024]), inc=vec)
                                kv_guard[sub % 2] = nvv
                                sc.wait("sync", vec, nvv)
                                dma(sc, "sync", st, ka_buf[row0:row0 + 128, :],
                                    t_kvbf[:, 0:512])
                                nst = dma(sc, "sync", st,
                                          va_buf[row0:row0 + 128, :],
                                          t_kvbf[:, 512:1024])
                            else:
                                for jq in range(2):
                                    sc.on("vector", lambda eng, jq=jq,
                                          ps_kv=ps_kv: eng.tensor_tensor(
                                              t_kf[:, jq * 512:(jq + 1) * 512],
                                              ps_kv[:, jq * 512:(jq + 1) * 512],
                                              t_bkv[:, jq * 512:(jq + 1) * 512],
                                              ALU.add))
                                for hh in range(4):
                                    b0 = hh * 128
                                    xr = t_kf[:, b0:b0 + 128:2]
                                    xi = t_kf[:, b0 + 1:b0 + 128:2]
                                    orr = t_rk[:, b0:b0 + 128:2]
                                    oi = t_rk[:, b0 + 1:b0 + 128:2]
                                    q1 = t_sqs[:, 0:64]
                                    q2 = t_sqs[:, 64:128]
                                    sc.on("vector", lambda eng, a=xr, o=q1:
                                          eng.tensor_tensor(o, a, t_ck[:, :],
                                                            ALU.mult))
                                    sc.on("vector", lambda eng, a=xi, o=q2:
                                          eng.tensor_tensor(o, a, t_sk[:, :],
                                                            ALU.mult))
                                    sc.on("vector", lambda eng, o=orr, a=q1,
                                          b2=q2: eng.tensor_tensor(
                                              o, a, b2, ALU.subtract))
                                    sc.on("vector", lambda eng, a=xr, o=q1:
                                          eng.tensor_tensor(o, a, t_sk[:, :],
                                                            ALU.mult))
                                    sc.on("vector", lambda eng, a=xi, o=q2:
                                          eng.tensor_tensor(o, a, t_ck[:, :],
                                                            ALU.mult))
                                    sc.on("vector", lambda eng, o=oi, a=q1,
                                          b2=q2: eng.tensor_tensor(
                                              o, a, b2, ALU.add))
                                sc.on("vector", lambda eng: eng.tensor_copy(
                                    t_kvbf[:, 0:512], t_rk[:, :]))
                                nvv = sc.on("vector", lambda eng,
                                            ps_kv=ps_kv: eng.tensor_tensor(
                                                t_kvbf[:, 512:1024],
                                                ps_kv[:, 512:1024],
                                                t_bkv[:, 512:1024], ALU.add),
                                            inc=vec)
                                kv_guard[sub % 2] = nvv
                                sc.wait("sync", vec, nvv)
                                dma(sc, "sync", st, kb_own[row0:row0 + 128, :],
                                    t_kvbf[:, 0:512])
                                nst = dma(sc, "sync", st,
                                          vb_own[row0:row0 + 128, :],
                                          t_kvbf[:, 512:1024])

                            if is_own and FL["do_q"]:
                                if variant == "a":
                                    for qc in range(2):
                                        sc.wait("vector", mm_s, nmq[qc])
                                        for jq in range(2):
                                            nvq = sc.on(
                                                "vector", lambda eng, jq=jq,
                                                qc=qc: eng.tensor_copy(
                                                    t_qf[:, qc * 1024 + jq * 512:
                                                         qc * 1024 + jq * 512
                                                         + 512],
                                                    ps_q2[qc][:, jq * 512:
                                                              (jq + 1) * 512]),
                                                inc=(vec if jq == 1 else None))
                                        q_guard[qc] = nvq
                                    sc.wait("scalar", vec, nvq)
                                    for hh in range(16):
                                        sc.on("scalar", lambda eng, hh=hh:
                                              eng.activation(
                                                  t_sqs[:, hh * 128:
                                                        (hh + 1) * 128],
                                                  t_qf[:, hh * 128:
                                                       (hh + 1) * 128],
                                                  AF.Square,
                                                  accum_out=t_ss[:, hh:hh + 1]))
                                    sc.chain("scalar", act)
                                    naq = sc.on("scalar", lambda eng:
                                                eng.activation(
                                                    t_sd[:, 0:16], t_ss[:, 0:16],
                                                    AF.Sqrt, scale=1.0 / 128.0,
                                                    bias=RMS_EPS_QK), inc=act)
                                    sc.wait("vector", act, naq)
                                    sc.on("vector", lambda eng: eng.reciprocal(
                                        t_rst[:, 0:16], t_sd[:, 0:16]))
                                    sc.chain("vector", vec)
                                    for hh in range(16):
                                        b0 = hh * 128
                                        x1a = t_qf[:, b0:b0 + 64]
                                        x2a = t_qf[:, b0 + 64:b0 + 128]
                                        o1 = t_rr[:, b0:b0 + 64]
                                        o2 = t_rr[:, b0 + 64:b0 + 128]
                                        sc.on("vector", lambda eng, a=x1a, o=o1:
                                              eng.tensor_tensor(
                                                  o, a, t_cq[:, :], ALU.mult))
                                        sc.on("vector", lambda eng, a=x2a, o=o2:
                                              eng.tensor_tensor(
                                                  o, a, t_sq_[:, :], ALU.mult))
                                        sc.on("vector", lambda eng, o=o1, o2=o2:
                                              eng.tensor_tensor(o, o, o2,
                                                                ALU.add))
                                        sc.on("vector", lambda eng, a=x2a, o=o2:
                                              eng.tensor_tensor(
                                                  o, a, t_cq[:, :], ALU.mult))
                                        sc.on("vector", lambda eng, a=x1a, s=x2a:
                                              eng.tensor_tensor(
                                                  s, a, t_sq_[:, :], ALU.mult))
                                        sc.on("vector", lambda eng, o=o2, s=x2a:
                                              eng.tensor_tensor(o, o, s,
                                                                ALU.subtract))
                                        sc.on("vector", lambda eng, hh=hh, b0=b0:
                                              eng.tensor_scalar(
                                                  t_obf[:, b0:b0 + 128],
                                                  t_rr[:, b0:b0 + 128],
                                                  t_rst[:, hh:hh + 1], None,
                                                  ALU.mult),
                                              inc=(vec if hh == 15 else None))
                                else:
                                    for qc in range(2):
                                        sc.wait("vector", mm_s, nmq[qc])
                                        for jq in range(2):
                                            nvq = sc.on(
                                                "vector", lambda eng, jq=jq,
                                                qc=qc: eng.tensor_tensor(
                                                    t_qf[:, qc * 1024 + jq * 512:
                                                         qc * 1024 + jq * 512
                                                         + 512],
                                                    ps_q2[qc][:, jq * 512:
                                                              (jq + 1) * 512],
                                                    t_bq[:, qc * 1024 + jq * 512:
                                                         qc * 1024 + jq * 512
                                                         + 512],
                                                    ALU.add),
                                                inc=(vec if jq == 1 else None))
                                        q_guard[qc] = nvq
                                    for hh in range(16):
                                        b0 = hh * 128
                                        xr = t_qf[:, b0:b0 + 128:2]
                                        xi = t_qf[:, b0 + 1:b0 + 128:2]
                                        orr = t_rr[:, b0:b0 + 128:2]
                                        oi = t_rr[:, b0 + 1:b0 + 128:2]
                                        q1 = t_sqs[:, 0:64]
                                        q2 = t_sqs[:, 64:128]
                                        sc.on("vector", lambda eng, a=xr, o=q1:
                                              eng.tensor_tensor(
                                                  o, a, t_ck[:, :], ALU.mult))
                                        sc.on("vector", lambda eng, a=xi, o=q2:
                                              eng.tensor_tensor(
                                                  o, a, t_sk[:, :], ALU.mult))
                                        sc.on("vector", lambda eng, o=orr, a=q1,
                                              b2=q2: eng.tensor_tensor(
                                                  o, a, b2, ALU.subtract))
                                        sc.on("vector", lambda eng, a=xr, o=q1:
                                              eng.tensor_tensor(
                                                  o, a, t_sk[:, :], ALU.mult))
                                        sc.on("vector", lambda eng, a=xi, o=q2:
                                              eng.tensor_tensor(
                                                  o, a, t_ck[:, :], ALU.mult))
                                        sc.on("vector", lambda eng, o=oi, a=q1,
                                              b2=q2: eng.tensor_tensor(
                                                  o, a, b2, ALU.add))
                                    nvq2 = sc.on("vector", lambda eng:
                                                 eng.tensor_copy(
                                                     t_obf[:, :], t_rr[:, :]),
                                                 inc=vec)
                                nvq2 = vec.n
                                sc.wait("sync", vec, nvq2)
                                qdst = qa if variant == "a" else qb
                                nst = dma(sc, "sync", st,
                                          qdst[own_row0:own_row0 + 128, :],
                                          t_obf[:, :])
                            fs_t = sflag(sc)
                            sc.wait("vector", stf, fs_t)
                        xT_guard[blk % 2] = mm_s.n
                    sc.wait("sync", st, st.n)

            sc = Script()
            start_next(sc)
            sc.wait("sync", cw, cw.n)
            plan_rope_prep(sc, "a")
            start_next = barrier(sc)
            phases.append(sc)
            check_done()

            def plan_attn(sc, variant):
                qsrc = qa if variant == "a" else qb
                ksrc = ka_buf if variant == "a" else ag_k
                vsrc = va_buf if variant == "a" else ag_v
                msrc = mask_a if variant == "a" else mask_b
                wo_d = wbf["wo_a_t"][1] if variant == "a" else wbf["wo_b"][1]
                xin = x_own if variant == "a" else x2
                xout = x1 if variant == "a" else x3
                kcols = NB * 1024 if variant == "a" else 2 * TPC
                with contextlib.ExitStack() as ats:
                    aec = ats.enter_context
                    t_kT = aec(nc.sbuf_tensor(U("at_kT"), [128, 4 * kcols],
                                              BF16))
                    t_v = aec(nc.sbuf_tensor(
                        U("at_v"), [128, (kcols // 128) * 512], BF16))
                    t_wo = aec(nc.sbuf_tensor(U("at_wo"), [128, 16 * 2048],
                                              BF16))
                    t_mask = aec(nc.sbuf_tensor(U("at_mask"), [128, 8 * 512],
                                                BF16))
                    t_qT = aec(nc.sbuf_tensor(U("at_qT"), [128, 2 * 512],
                                              BF16))
                    t_p = aec(nc.sbuf_tensor(U("at_p"), [128, 12 * 512], BF16))
                    t_rl = aec(nc.sbuf_tensor(U("at_rl"), [128, 512], F32))
                    t_oall = aec(nc.sbuf_tensor(U("at_oall"),
                                                [128, 16 * 512], BF16))
                    t_xold2 = [aec(nc.sbuf_tensor(U("at_xold"),
                                                  [128, 2048], F32))
                               for _ in range(2)]
                    t_xnew2 = [aec(nc.sbuf_tensor(U("at_xnew"),
                                                  [128, 2048], F32))
                               for _ in range(2)]
                    t_bo = (aec(nc.sbuf_tensor(U("at_bo"), [128, 2048],
                                                BF16))
                            if variant == "b" else None)
                    ps_s = [aec(nc.psum_tensor(U("at_ps_s"), [128, 1024],
                                               F32)) for _ in range(2)]
                    ps_l = aec(nc.psum_tensor(U("at_ps_l"), [128, 512], F32))
                    ps_o = aec(nc.psum_tensor(U("at_ps_o"), [128, 512], F32))
                    ps_w = aec(nc.psum_tensor(U("at_ps_w"), [128, 1024], F32))

                    for kv in range(4):
                        for c0 in range(0, kcols, 2048):
                            w = min(2048, kcols - c0)
                            dma(sc, "sync", ld,
                                t_kT[:, kv * kcols + c0:kv * kcols + c0 + w],
                                ksrc[c0:c0 + w, kv * 128:(kv + 1) * 128],
                                transpose=True)
                    for r0 in range(0, kcols, 512):
                        v0 = vsrc[r0:r0 + 512, :]
                        vsrc_r = AP(v0.tensor, v0.offset,
                                    [[512, 128], [512 * 128, 4], [1, 512]])
                        dma(sc, "sync", ld,
                            t_v[:, (r0 // 128) * 512:
                                (r0 // 128) * 512 + 4 * 512],
                            vsrc_r)
                    for k in range(16):
                        dma(sc, "gpsimd", cc,
                            t_wo[:, k * 2048:(k + 1) * 2048],
                            wo_d[k * 128:(k + 1) * 128, :])
                    fw_wo = cflag(sc)
                    fl_kv = lflag(sc)
                    if variant == "b":
                        if True:
                            t_bor = t_xnew2[1][0:1, 0:2048]
                            dma(sc, "sync", ld, t_bor, bo_b[:, :])
                            flbo = lflag(sc)
                            sc.wait("tensor", ldf, flbo)
                            for j in range(4):
                                nm = sc.on("tensor", lambda eng, j=j:
                                           eng.matmul(
                                               ps_w[:, 0:512],
                                               sb_ones_f32[:, :],
                                               t_bor[:, j * 512:
                                                     (j + 1) * 512],
                                               start=True, stop=True),
                                           inc=mm_s)
                                sc.wait("vector", mm_s, nm)
                                nv = sc.on("vector", lambda eng, j=j:
                                           eng.tensor_copy(
                                               t_bo[:, j * 512:
                                                    (j + 1) * 512],
                                               ps_w[:, 0:512]), inc=vec)
                                sc.wait("tensor", vec, nv)

                    def kpos(kt):
                        if variant == "a":
                            return kt * 128
                        g, sub = divmod(kt, 4)
                        return (g % 2) * 2048 + (g // 2) * 512 + sub * 128

                    qT_guard = [0, 0]
                    for blk in range(NB):
                        if blk == 1 and variant == "a":
                            for s_, d_ in ((w_gu_sh, w_gu),
                                           (w_dn_sh, w_dn)):
                                sc.on("gpsimd", lambda eng, s=s_, d=d_:
                                      eng.collective_compute(
                                          "AllGather", ALU.bypass,
                                          replica_groups=[list(range(8))],
                                          ins=[s.opt()], outs=[d.opt()]),
                                      inc=col)
                        sc.wait("sync", vec, vec.n)
                        for sl in range(8):
                            dma(sc, "sync", ld,
                                t_mask[:, sl * 512:(sl + 1) * 512],
                                msrc[blk, sl, :, :])
                        fl_mk = lflag(sc)
                        if variant == "a":
                            ktiles = [blk * 8 + j for j in range(8)]
                            mask_slot = {blk * 8 + j: j for j in range(8)}
                        else:
                            ktiles = list(range(8 * blk + 8))
                            mask_slot = {8 * blk + j: j for j in range(8)}
                        pairs = [ktiles[i:i + 2]
                                 for i in range(0, len(ktiles), 2)]
                        sc.wait("vector", mm_s, mm_s.n)
                        qflags = {}

                        def load_q(h2):
                            qb2 = h2 % 2
                            dma(sc, "sync", ld,
                                t_qT[:, qb2 * 512:(qb2 + 1) * 512],
                                qsrc[blk * 512:(blk + 1) * 512,
                                     h2 * 128:(h2 + 1) * 128],
                                transpose=True)
                            qflags[h2] = lflag(sc)

                        load_q(0)
                        nexp = [0, 0, 0]
                        ncons = [0, 0, 0]
                        npool = {}
                        nvec_evac = 0
                        for hh in range(16):
                            kv = hh // GRP
                            qb_i = hh % 2
                            if True:
                                if hh + 1 < 16:
                                    sc.wait("sync", mm_s,
                                            qT_guard[(hh + 1) % 2])
                                    load_q(hh + 1)
                                sc.wait("tensor", ldf, max(qflags[hh],
                                                           fl_kv, fl_mk))
                                qs_ap = t_qT[:, qb_i * 512:(qb_i + 1) * 512]
                                ROT = 3

                                def spsum_of(pi):
                                    return (ps_s[0], ps_s[1], ps_w)[pi % ROT]

                                def pslot_of(pi, ji):
                                    return t_p[:, ((pi % ROT) * 2 + ji) * 512:
                                               ((pi % ROT) * 2 + ji) * 512
                                               + 512]

                                def mslot_of(pi, ji):
                                    return t_p[:, (6 + (pi % ROT) * 2 + ji)
                                               * 512:
                                               (6 + (pi % ROT) * 2 + ji)
                                               * 512 + 512]

                                state = {}

                                def emit_S(pi):
                                    pair = pairs[pi]
                                    sp = spsum_of(pi)
                                    pk = pi - ROT
                                    if pk >= 0 and ("exp", pk) in state:
                                        sc.wait("tensor", act,
                                                state[("exp", pk)])
                                    elif pk < 0 and nexp[pi % ROT]:
                                        sc.wait("tensor", act,
                                                nexp[pi % ROT])
                                    for ji, kt in enumerate(pair):
                                        kc = kpos(kt)
                                        nm = sc.on(
                                            "tensor", lambda eng, kv=kv,
                                            kc=kc, ji=ji, sp=sp,
                                            qs_ap=qs_ap: eng.matmul(
                                                sp[:, ji * 512:
                                                   (ji + 1) * 512],
                                                t_kT[:, kv * kcols + kc:
                                                     kv * kcols + kc + 128],
                                                qs_ap, start=True,
                                                stop=True),
                                            inc=mm_s)
                                    state[("S", pi)] = nm

                                def emit_exp(pi):
                                    pair = pairs[pi]
                                    sp = spsum_of(pi)
                                    wexp = 512 * len(pair)
                                    psl = t_p[:, (pi % ROT) * 1024:
                                              (pi % ROT) * 1024 + wexp]
                                    sc.wait("scalar", mm_s, state[("S", pi)])
                                    pk = pi - ROT
                                    if pk >= 0 and ("av", pk) in state:
                                        sc.wait("scalar", mm_s,
                                                state[("av", pk)])
                                        sc.wait("scalar", pl,
                                                state.get(("pool", pk), 0))
                                    elif pk < 0 and ncons[pi % ROT]:
                                        sc.wait("scalar", mm_s,
                                                ncons[pi % ROT])
                                        sc.wait("scalar", pl,
                                                npool.get(pi % ROT, 0))
                                    if variant == "a":
                                        na = sc.on(
                                            "scalar",
                                            lambda eng, hh=hh, sp=sp, p=psl,
                                            w=wexp: eng.activation(
                                                p, sp[:, 0:w], AF.Exp,
                                                bias=sb_gbias[:, hh:hh + 1],
                                                scale=sb_gsc[:, hh:hh + 1]),
                                            inc=act)
                                    else:
                                        na = sc.on(
                                            "scalar",
                                            lambda eng, sp=sp, p=psl,
                                            w=wexp: eng.activation(
                                                p, sp[:, 0:w], AF.Exp,
                                                bias=-SHIFT_B), inc=act)
                                    state[("exp", pi)] = na
                                    mx = 0
                                    uses = []
                                    for ji, kt in enumerate(pair):
                                        if kt in mask_slot:
                                            sl_ = mask_slot[kt]
                                            m = mslot_of(pi, ji)
                                            p0 = pslot_of(pi, ji)
                                            sc.wait("gpsimd", act, na)
                                            mx = sc.on(
                                                "gpsimd",
                                                lambda eng, p=p0, m=m,
                                                sl_=sl_: eng.tensor_tensor(
                                                    m, p,
                                                    t_mask[:, sl_ * 512:
                                                           (sl_ + 1) * 512],
                                                    ALU.mult), inc=pl)
                                            uses.append(m)
                                        else:
                                            uses.append(pslot_of(pi, ji))
                                    state[("pool", pi)] = mx
                                    state[("use", pi)] = uses

                                def emit_lav(pi):
                                    pair = pairs[pi]
                                    if state.get(("pool", pi)):
                                        sc.wait("tensor", pl,
                                                state[("pool", pi)])
                                    else:
                                        sc.wait("tensor", act,
                                                state[("exp", pi)])
                                    if pi == 0 and nvec_evac:
                                        sc.wait("tensor", vec, nvec_evac)
                                    for ji, kt in enumerate(pair):
                                        kc = kpos(kt)
                                        pap = state[("use", pi)][ji]
                                        f = (pi == 0 and ji == 0)
                                        la = (kt == ktiles[-1])
                                        sc.on("tensor", lambda eng, p=pap,
                                              f=f, la=la: eng.matmul(
                                                  ps_l[:, :],
                                                  sb_ones[:, :], p,
                                                  start=f, stop=la))
                                        nc2 = sc.on(
                                            "tensor", lambda eng, p=pap,
                                            kv=kv, kc=kc, f=f, la=la:
                                            eng.matmul(
                                                ps_o[:, :],
                                                t_v[:, (kc // 128) * 512 +
                                                    kv * 128:
                                                    (kc // 128) * 512 +
                                                    kv * 128 + 128],
                                                p, start=f, stop=la),
                                            inc=mm_s)
                                    state[("av", pi)] = nc2

                                np_ = len(pairs)
                                emit_S(0)
                                if np_ > 1:
                                    emit_S(1)
                                emit_exp(0)
                                for pi in range(np_):
                                    if pi + 2 < np_:
                                        emit_S(pi + 2)
                                    if pi + 1 < np_:
                                        emit_exp(pi + 1)
                                    emit_lav(pi)
                                nexp = [0, 0, 0]
                                ncons = [0, 0, 0]
                                npool = {}
                                for pi in range(max(0, np_ - ROT), np_):
                                    nexp[pi % ROT] = state[("exp", pi)]
                                    ncons[pi % ROT] = state[("av", pi)]
                                    npool[pi % ROT] = state.get(("pool", pi),
                                                                0)
                                qT_guard[qb_i] = mm_s.n
                                nmm_done = mm_s.n
                                sc.wait("vector", mm_s, nmm_done)
                                sc.on("vector", lambda eng: eng.reciprocal(
                                    t_rl[:, :], ps_l[:, :]))
                                if variant == "b":
                                    sc.on("vector", lambda eng:
                                          eng.tensor_scalar(
                                              t_rl[:, :], t_rl[:, :],
                                              2048.0 ** -0.5, None,
                                              ALU.mult))
                                sc.chain("vector", vec)
                                nvec_evac = sc.on(
                                    "vector", lambda eng, hh=hh:
                                    eng.tensor_tensor(
                                        t_oall[:, hh * 512:(hh + 1) * 512],
                                        ps_o[:, :], t_rl[:, :],
                                        ALU.mult), inc=vec)
                        sc.wait("tensor", vec, vec.n)
                        sc.wait("tensor", ccf, fw_wo)
                        wo_ev = [0, 0]
                        xo_flag = {}

                        def load_xold(q2):
                            dma(sc, "sync", ld, t_xold2[q2 % 2][:, :],
                                xin[blk * 512 + q2 * 128:
                                    blk * 512 + q2 * 128 + 128, :])
                            xo_flag[q2] = lflag(sc)

                        load_xold(0)
                        for qs_i in range(4):
                            row0 = blk * 512 + qs_i * 128
                            if qs_i + 1 < 4:
                                load_xold(qs_i + 1)

                            def wchunk(e, qs_i=qs_i):
                                if qs_i % 2 == 0:
                                    return (ps_s[0][:, e * 512:
                                                    (e + 1) * 512]
                                            if e < 2 else
                                            ps_w[:, (e - 2) * 512:
                                                 (e - 2) * 512 + 512])
                                return (ps_s[1][:, e * 512:(e + 1) * 512]
                                        if e < 2 else
                                        (ps_l[:, :] if e == 2
                                         else ps_o[:, :]))
                            if wo_ev[qs_i % 2]:
                                sc.wait("tensor", vec, wo_ev[qs_i % 2])
                            for e in range(4):
                                tgt = wchunk(e)
                                for hh2 in range(16):
                                    nmw = sc.on(
                                        "tensor", lambda eng, hh2=hh2, e=e,
                                        qs_i=qs_i, tgt=tgt: eng.matmul(
                                            tgt,
                                            t_oall[:, hh2 * 512 +
                                                   qs_i * 128:
                                                   hh2 * 512 + qs_i * 128
                                                   + 128],
                                            t_wo[:, hh2 * 2048 + e * 512:
                                                 hh2 * 2048 + e * 512
                                                 + 512],
                                            start=(hh2 == 0),
                                            stop=(hh2 == 15)),
                                        inc=(mm_s if hh2 == 15 else None))
                            sc.wait("vector", mm_s, nmw)
                            sc.wait("vector", ldf, xo_flag[qs_i])
                            sc.wait("vector", stf, stf.n)
                            xoq = t_xold2[qs_i % 2]
                            xnq = t_xnew2[qs_i % 2]
                            for e in range(4):
                                tgt = wchunk(e)
                                if variant == "b":
                                    sc.on("vector", lambda eng, e=e,
                                          tgt=tgt, xnq=xnq:
                                          eng.tensor_tensor(
                                              xnq[:, e * 512:(e + 1) * 512],
                                              tgt,
                                              t_bo[:, e * 512:
                                                   (e + 1) * 512],
                                              ALU.add))
                                    nvw = sc.on(
                                        "vector", lambda eng, e=e, xnq=xnq,
                                        xoq=xoq: eng.tensor_tensor(
                                            xnq[:, e * 512:(e + 1) * 512],
                                            xnq[:, e * 512:(e + 1) * 512],
                                            xoq[:, e * 512:(e + 1) * 512],
                                            ALU.add),
                                        inc=(vec if e == 3 else None))
                                else:
                                    nvw = sc.on(
                                        "vector", lambda eng, e=e, tgt=tgt,
                                        xnq=xnq, xoq=xoq:
                                        eng.tensor_tensor(
                                            xnq[:, e * 512:(e + 1) * 512],
                                            tgt,
                                            xoq[:, e * 512:(e + 1) * 512],
                                            ALU.add),
                                        inc=(vec if e == 3 else None))
                            wo_ev[qs_i % 2] = nvw
                            sc.wait("sync", vec, nvw)
                            dma(sc, "sync", st, xout[row0:row0 + 128, :],
                                xnq[:, :])
                            sflag(sc)
                    sc.wait("sync", st, st.n)

            sc = Script()
            start_next(sc)
            sc.wait("sync", cw, cw.n)
            plan_attn(sc, "a")
            start_next = barrier(sc)
            phases.append(sc)
            check_done()

            sc = Script()
            start_next(sc)
            plan_ln(sc, lambda i: x1[i * 128:(i + 1) * 128, :], xln, 16)
            start_next = barrier(sc)
            phases.append(sc)
            check_done()

            def plan_ffn(sc, xres_ap, xout_ap):
                GROUP = 4
                LW = GROUP * 128
                NGRP = 44 // GROUP
                with contextlib.ExitStack() as fs:
                    fec = fs.enter_context
                    t_xT = fec(nc.sbuf_tensor(U("ff_xT"), [128, 16 * 1024],
                                              BF16))
                    t_wg = [fec(nc.sbuf_tensor(U("ff_wg"),
                                               [128, 16 * 512 * GROUP // 4],
                                               BF16)) for _ in range(2)]
                    t_hh = fec(nc.sbuf_tensor(U("ff_h"), [128, 45 * 1024],
                                              BF16))
                    t_wd = [fec(nc.sbuf_tensor(U("ff_wd"), [128, 22 * 512],
                                               BF16)) for _ in range(2)]
                    t_y = fec(nc.sbuf_tensor(U("ff_y"), [128, 512], F32))
                    t_ss = fec(nc.sbuf_tensor(U("ff_ss"), [128, 4], F32))
                    t_ssum = fec(nc.sbuf_tensor(U("ff_ssum"), [128, 1], F32))
                    t_sd = fec(nc.sbuf_tensor(U("ff_sd"), [128, 1], F32))
                    t_rs = fec(nc.sbuf_tensor(U("ff_rs"), [128, 1], F32))
                    t_sq = fec(nc.sbuf_tensor(U("ff_sq"), [128, 512], BF16))
                    t_ss4 = fec(nc.sbuf_tensor(U("ff_ss4"), [128, 32], F32))
                    t_sm8 = fec(nc.sbuf_tensor(U("ff_sm8"), [128, 8], F32))
                    t_sd8 = fec(nc.sbuf_tensor(U("ff_sd8"), [128, 8], F32))
                    t_rs8 = fec(nc.sbuf_tensor(U("ff_rs8"), [128, 8], F32))
                    t_xc1 = t_wg[1][:, 0:2048].bitcast(F32)
                    t_xc2 = t_wg[1][:, 2048:4096].bitcast(F32)
                    t_y2 = t_wd[0][:, 0:4096].bitcast(F32)
                    t_xo = t_wd[0][:, 4096:8192].bitcast(F32)
                    ps = [fec(nc.psum_tensor(U("ff_ps"), [128, 1024], F32))
                          for _ in range(4)]

                    def hslot(i):
                        return t_hh[:, ((i + 44) % 45) * 1024:
                                    ((i + 44) % 45) * 1024 + 1024]

                    def hprev(i):
                        return t_hh[:, (i % 45) * 1024:(i % 45) * 1024 + 1024]

                    carry = {}
                    for half in range(2):
                        tok0 = half * 1024
                        gflag = {}
                        nmg_hist = {}

                        def load_wgroup(g, gflag=gflag):
                            buf = t_wg[g % 2]
                            row0 = ((g // NGRP) * I_FF
                                    + (g % NGRP) * GROUP * 128)
                            for k in range(16):
                                dma(sc, "sync", ld,
                                    buf[:, k * LW:(k + 1) * LW],
                                    w_gu[row0:row0 + LW,
                                         k * 128:(k + 1) * 128],
                                    transpose=True)
                            gflag[g] = lflag(sc)

                        if half == 0:
                            fence(sc)
                            for k in range(16):
                                dma(sc, "sync", ld,
                                    t_xT[:, k * 1024:(k + 1) * 1024],
                                    xln[tok0:tok0 + 1024,
                                        k * 128:(k + 1) * 128],
                                    transpose=True)
                            load_wgroup(0)
                        else:
                            gflag[0] = carry["g0"]

                        wdflag = {}

                        def load_wd(eq, seg, wdflag=wdflag):
                            buf = t_wd[(2 * eq + seg) % 2]
                            for ii in range(22):
                                i = seg * 22 + ii
                                dma(sc, "sync", ld,
                                    buf[:, ii * 512:(ii + 1) * 512],
                                    w_dn[eq * 512:(eq + 1) * 512,
                                         i * 128:(i + 1) * 128],
                                    transpose=True)
                            wdflag[(eq, seg)] = lflag(sc)

                        for g in range(2 * NGRP):
                            stage = g // NGRP
                            buf = t_wg[g % 2]
                            if g == 2 * NGRP - 2:
                                sc.wait("sync", st, st.n)
                                load_wd(0, 0)
                            if g + 1 < 2 * NGRP and (g + 1) not in gflag:
                                if g - 1 in nmg_hist:
                                    sc.wait("sync", mm_s, nmg_hist[g - 1])
                                if half > 0 and g == 0:
                                    sc.wait("sync", st, st.n)
                                load_wgroup(g + 1)
                            sc.wait("tensor", ldf, gflag[g])
                            if half > 0 and g == 0:
                                sc.wait("tensor", vec, carry["nvy_last"])
                            for ig in range(GROUP):
                                i = (g % NGRP) * GROUP + ig
                                psx = ps[2 * stage + (i % 2)]
                                if stage == 0:
                                    key = ("silu", i - 2)
                                    if key in nmg_hist:
                                        sc.wait("tensor", act, nmg_hist[key])
                                else:
                                    key = ("up", i - 2)
                                    if key in nmg_hist:
                                        sc.wait("tensor", vec, nmg_hist[key])
                                for k in range(16):
                                    for j in range(2):
                                        nmg = sc.on(
                                            "tensor", lambda eng, k=k, j=j,
                                            ig=ig, psx=psx, buf=buf:
                                            eng.matmul(
                                                psx[:, j * 512:(j + 1) * 512],
                                                buf[:, k * LW + ig * 128:
                                                    k * LW + ig * 128 + 128],
                                                t_xT[:, k * 1024 + j * 512:
                                                     k * 1024 + j * 512 + 512],
                                                start=(k == 0),
                                                stop=(k == 15)),
                                            inc=(mm_s if k == 15 and j == 1
                                                 else None))
                                nmg_hist[g] = nmg
                                if stage == 0:
                                    if half > 0 and i == 0:
                                        sc.wait("scalar", mm_s, carry["nmd6"])
                                    if half > 0 and i == 21:
                                        sc.wait("scalar", mm_s, carry["nmd7"])
                                    sc.wait("scalar", mm_s, nmg)
                                    nmg_hist[("silu", i)] = sc.on(
                                        "scalar", lambda eng, i=i, psx=psx:
                                        eng.activation(hprev(i), psx[:, :],
                                                       AF.Silu), inc=act)
                                else:
                                    sc.wait("vector", mm_s, nmg)
                                    for jq in range(2):
                                        nmg_hist[("up", i)] = sc.on(
                                            "vector",
                                            lambda eng, i=i, jq=jq, psx=psx:
                                            eng.tensor_tensor(
                                                hslot(i)[:, jq * 512:
                                                         (jq + 1) * 512],
                                                hprev(i)[:, jq * 512:
                                                         (jq + 1) * 512],
                                                psx[:, jq * 512:
                                                    (jq + 1) * 512],
                                                ALU.mult),
                                            inc=(vec if jq == 1 else None))
                        if half == 0:
                            sc.wait("sync", mm_s, nmg_hist[2 * NGRP - 1])
                            for k in range(16):
                                dma(sc, "sync", ld,
                                    t_xT[:, k * 1024:(k + 1) * 1024],
                                    xln[1024:2048,
                                        k * 128:(k + 1) * 128],
                                    transpose=True)
                            for k in range(16):
                                dma(sc, "sync", ld,
                                    t_wg[0][:, k * LW:(k + 1) * LW],
                                    w_gu[0:LW, k * 128:(k + 1) * 128],
                                    transpose=True)
                            carry["g0"] = lflag(sc)
                        sc.wait("tensor", vec, nmg_hist[("up", 43)])
                        sc.wait("tensor", act, nmg_hist[("silu", 43)])
                        nvy = {}
                        nvya = {}
                        nmd_hist = {}
                        nsq_last = 0
                        tail_stf = 0
                        for eq in range(4):
                            for seg in range(2):
                                buf = t_wd[(2 * eq + seg) % 2]
                                nxt = (eq, 1) if seg == 0 else (eq + 1, 0)
                                if nxt[0] < 4:
                                    idx = 2 * eq + seg
                                    if idx >= 1 and (idx - 1) in nmd_hist:
                                        sc.wait("sync", mm_s,
                                                nmd_hist[idx - 1])
                                    load_wd(*nxt)
                                sc.wait("tensor", ldf, wdflag[(eq, seg)])
                                for ts_i in range(8):
                                    psd = ps[ts_i // 2][
                                        :, (ts_i % 2) * 512:
                                        (ts_i % 2) * 512 + 512]
                                    if seg == 0 and (eq, ts_i) in nvy:
                                        sc.wait("tensor", vec,
                                                nvy.pop((eq, ts_i), 0))
                                    if seg == 0 and (eq - 1, ts_i) in nvy:
                                        sc.wait("tensor", vec,
                                                nvy.pop((eq - 1, ts_i)))
                                    for ii in range(22):
                                        i = seg * 22 + ii
                                        nmd = sc.on(
                                            "tensor", lambda eng, i=i, ii=ii,
                                            ts_i=ts_i, psd=psd, buf=buf:
                                            eng.matmul(
                                                psd,
                                                hslot(i)[:, ts_i * 128:
                                                         ts_i * 128 + 128],
                                                buf[:, ii * 512:
                                                    (ii + 1) * 512],
                                                start=(i == 0),
                                                stop=(i == 43)),
                                            inc=(mm_s if ii == 21 else None))
                                    if seg == 1:
                                        sc.wait("vector", mm_s, nmd)
                                        sc.wait("vector", stf, stf.n)
                                        if nsq_last:
                                            sc.wait("vector", act, nsq_last)
                                        nv = sc.on(
                                            "vector",
                                            lambda eng, psd=psd:
                                            eng.tensor_copy(t_y[:, 0:512],
                                                            psd), inc=vec)
                                        nvy[(eq, ts_i)] = nv
                                        sc.wait("scalar", vec, nv)
                                        nsq = sc.on(
                                            "scalar",
                                            lambda eng, eq=eq,
                                            ts_i=ts_i: eng.activation(
                                                t_sq[:, :], t_y[:, 0:512],
                                                AF.Square,
                                                accum_out=t_ss4[
                                                    :, 4 * ts_i + eq:
                                                    4 * ts_i + eq + 1]),
                                            inc=act)
                                        nsq_last = nsq
                                        sc.wait("sync", vec, nv)
                                        dma(sc, "sync", st,
                                            y_scr[ts_i * 128:
                                                  (ts_i + 1) * 128,
                                                  eq * 512:(eq + 1) * 512],
                                            t_y[:, 0:512])
                                        sflag(sc)
                                        if eq == 3:
                                            row0 = tok0 + ts_i * 128
                                            y2p = t_y2
                                            xop = t_xo
                                            sc.wait("sync", vec, vec.n)
                                            dma(sc, "sync", ld, y2p,
                                                y_scr[ts_i * 128:
                                                      (ts_i + 1) * 128, :])
                                            dma(sc, "sync", ld, xop,
                                                xres_ap[row0:row0 + 128, :])
                                            fl_y = lflag(sc)
                                            sc.wait("vector", act, nsq)
                                            nv3 = sc.on(
                                                "vector",
                                                lambda eng, ts_i=ts_i:
                                                eng.reduce_sum(
                                                    t_sm8[:, ts_i:ts_i + 1],
                                                    t_ss4[:, 4 * ts_i:
                                                          4 * ts_i + 4],
                                                    axis=mybir.AxisListType.X),
                                                inc=vec)
                                            sc.wait("scalar", vec, nv3)
                                            na2 = sc.on(
                                                "scalar",
                                                lambda eng, ts_i=ts_i:
                                                eng.activation(
                                                    t_sd8[:, ts_i:ts_i + 1],
                                                    t_sm8[:, ts_i:ts_i + 1],
                                                    AF.Sqrt, scale=1.0 / D,
                                                    bias=RMS_EPS_FFN),
                                                inc=act)
                                            sc.wait("vector", act, na2)
                                            sc.on("vector",
                                                  lambda eng, ts_i=ts_i:
                                                  eng.reciprocal(
                                                      t_rs8[:, ts_i:ts_i + 1],
                                                      t_sd8[:, ts_i:
                                                            ts_i + 1]))
                                            sc.chain("vector", vec)
                                            sc.wait("vector", ldf, fl_y)
                                            if tail_stf:
                                                sc.wait("vector", stf,
                                                        tail_stf)
                                            nv2 = 0
                                            for cc2 in range(2):
                                                xcc = (t_xc1 if cc2 == 0
                                                       else t_xc2)
                                                sc.on(
                                                    "vector",
                                                    lambda eng, xcc=xcc,
                                                    y2p=y2p, cc2=cc2,
                                                    ts_i=ts_i:
                                                    eng.tensor_scalar(
                                                        xcc, y2p[
                                                            :, cc2 * 1024:
                                                            cc2 * 1024
                                                            + 1024],
                                                        t_rs8[:, ts_i:
                                                              ts_i + 1],
                                                        None, ALU.mult))
                                                nv2 = sc.on(
                                                    "vector",
                                                    lambda eng, xcc=xcc,
                                                    xop=xop, cc2=cc2:
                                                    eng.tensor_tensor(
                                                        xcc, xcc, xop[
                                                            :, cc2 * 1024:
                                                            cc2 * 1024
                                                            + 1024],
                                                        ALU.add), inc=vec)
                                                sc.wait("sync", vec, nv2)
                                                dma(sc, "sync", st,
                                                    xout_ap[
                                                        row0:row0 + 128,
                                                        cc2 * 1024:
                                                        cc2 * 1024 + 1024],
                                                    xcc)
                                            tail_stf = sflag(sc)
                                nmd_hist[2 * eq + seg] = nmd
                        carry["nmd6"] = nmd_hist[6]
                        carry["nmd7"] = nmd_hist[7]
                        carry["nvy_last"] = nvy[(3, 7)]
                    sc.wait("sync", st, st.n)

            sc = Script()
            start_next(sc)
            sc.wait("sync", col, 2)
            plan_ffn(sc, x1, x2)
            start_next = barrier(sc)
            phases.append(sc)
            check_done()

            sc = Script()
            start_next(sc)
            plan_ln(sc, lambda i: x2[i * 128:(i + 1) * 128, :], xln, 16)
            start_next = barrier(sc)
            phases.append(sc)
            check_done()

            sc = Script()
            start_next(sc)
            sc.wait("sync", cw, cw.n)
            plan_rope_prep(sc, "b")
            sc.wait("gpsimd", st, st.n)
            for src, dst in ((kb_own, ag_k), (vb_own, ag_v)):
                sc.on("gpsimd", lambda eng, s=src, d=dst: eng.collective_compute(
                    "AllGather", ALU.bypass,
                    replica_groups=[[0, 1], [2, 3], [4, 5], [6, 7]],
                    ins=[s.opt()], outs=[d.opt()]), inc=cc)
            sc.wait("gpsimd", cc, cc.n)
            start_next = barrier(sc)
            phases.append(sc)
            check_done()

            sc = Script()
            start_next(sc)
            sc.wait("sync", cw, cw.n)
            plan_attn(sc, "b")
            start_next = barrier(sc)
            phases.append(sc)
            check_done()

            sc = Script()
            start_next(sc)
            plan_ln(sc, lambda i: x3[i * 128:(i + 1) * 128, :], xln, 16)
            start_next = barrier(sc)
            phases.append(sc)
            check_done()

            sc = Script()
            start_next(sc)
            plan_ffn(sc, x3, out)
            sc.wait("sync", st, st.n)
            barrier(sc)
            phases.append(sc)
            check_done()

        except _StopPlanning:
            pass
        if tap_map:
            sc = Script()
            for e in Script.ENGINES:
                sc.wait(e, ph, ph.n)
            for t, (src_, dst) in tap_map.items():
                rows = src_.shape[0]
                for r0 in range(0, rows, 512):
                    r1 = min(rows, r0 + 512)
                    dma(sc, "sync", st, dst[r0:r1, :], src_[r0:r1, :])
            sc.wait("sync", st, st.n)
            phases.append(sc)
        for sc_i in phases:
            with nc.Block() as blk_:
                sc_i.run(blk_)

    return nc


def _rope_tables(pos, inv):
    ang = pos[:, None].astype(np.float64) * inv[None, :]
    return (np.cos(ang).astype(np.float32), np.sin(ang).astype(np.float32))


def _core_inputs(core, full):
    b, p = core // 2, core % 2
    blocks = [p + 2 * bi for bi in range(NB)]
    own_rows = np.concatenate([np.arange(g * 512, g * 512 + 512)
                               for g in blocks])
    x = np.asarray(full["L0_attn_a_x"])
    xb = x[b]
    x_own = np.ascontiguousarray(xb[own_rows])
    halos, kpos = [], []
    qpos = own_rows.astype(np.int64)
    for g in blocks:
        h0 = g * 512 - 512
        if h0 < 0:
            halos.append(np.zeros((512, D), np.float32))
            kpos.append(np.zeros(512, np.int64))
        else:
            halos.append(xb[h0:h0 + 512])
            kpos.append(np.arange(h0, h0 + 512))
        kpos.append(np.arange(g * 512, g * 512 + 512))
    x_halo = np.ascontiguousarray(np.concatenate(halos, axis=0))
    kpos = np.concatenate(kpos)

    cos_aq, sin_aq = _rope_tables(qpos, ROPE_A_INV)
    cos_ak, sin_ak = _rope_tables(kpos, ROPE_A_INV)
    cos_b_, sin_b_ = _rope_tables(qpos, ROPE_B_INV)

    mask_a = np.zeros((NB, 8, 128, 512), np.float32)
    for bi, g in enumerate(blocks):
        for j in range(8):
            kg = (g * 512 - 512 + 128 * j) + np.arange(128)[:, None]
            qg = g * 512 + np.arange(512)[None, :]
            mask_a[bi, j] = ((qg - kg >= 0) & (qg - kg < WIN) & (kg >= 0))
    mask_b = np.zeros((NB, 8, 128, 512), np.float32)
    for bi, g in enumerate(blocks):
        for m in range(8):
            kt = 8 * bi + m
            kg = kt * 128 + np.arange(128)[:, None]
            qg = g * 512 + np.arange(512)[None, :]
            mask_b[bi, m] = (kg <= qg)

    return dict(
        x_own=x_own, x_halo=x_halo,
        wq_a_t=np.ascontiguousarray(np.asarray(full["L0_attn_a_Wq"]).T),
        wk_a_t=np.ascontiguousarray(np.asarray(full["L0_attn_a_Wk"]).T),
        wv_a_t=np.ascontiguousarray(np.asarray(full["L0_attn_a_Wv"]).T),
        wo_a_t=np.ascontiguousarray(np.asarray(full["L0_attn_a_Wo"]).T),
        qgain=np.ascontiguousarray(
            np.asarray(full["L0_attn_a_q_gain"]).reshape(1, 16)),
        wq_b=np.ascontiguousarray(np.asarray(full["L0_attn_b_Wq"])),
        bq_b=np.ascontiguousarray(
            np.asarray(full["L0_attn_b_bq"]).reshape(1, D)),
        wk_b=np.ascontiguousarray(np.asarray(full["L0_attn_b_Wk"])),
        bk_b=np.ascontiguousarray(
            np.asarray(full["L0_attn_b_bk"]).reshape(1, 512)),
        wv_b=np.ascontiguousarray(np.asarray(full["L0_attn_b_Wv"])),
        bv_b=np.ascontiguousarray(
            np.asarray(full["L0_attn_b_bv"]).reshape(1, 512)),
        wo_b=np.ascontiguousarray(np.asarray(full["L0_attn_b_Wo"])),
        bo_b=np.ascontiguousarray(
            np.asarray(full["L0_attn_b_bo"]).reshape(1, D)),
        gup_pk=np.ascontiguousarray(
            np.asarray(full["L0_ffn_gate_up_packed"])[
                core * (2 * I_FF // 8):(core + 1) * (2 * I_FF // 8)]),
        gup_sct=np.ascontiguousarray(
            np.asarray(full["L0_ffn_gate_up_scales"]).T[
                core * (2 * I_FF // 8):(core + 1) * (2 * I_FF // 8)]),
        dn_pk=np.ascontiguousarray(
            np.asarray(full["L0_ffn_down_packed"])[
                core * (D // 8):(core + 1) * (D // 8)]),
        dn_sct=np.ascontiguousarray(
            np.asarray(full["L0_ffn_down_scales"]).T[
                core * (D // 8):(core + 1) * (D // 8)]),
        cos_aq=cos_aq, sin_aq=sin_aq, cos_ak=cos_ak, sin_ak=sin_ak,
        cos_b=cos_b_, sin_b=sin_b_,
        mask_a=mask_a.astype(BF), mask_b=mask_b.astype(BF),
    )


_NC_CACHE = {}


def kernel(**inputs):
    taps = tuple(DEBUG_TAPS)
    key = (taps, PHASE_LIMIT)
    if key not in _NC_CACHE:
        _NC_CACHE[key] = build_nc(taps, PHASE_LIMIT)
    nc = _NC_CACHE[key]
    in_maps = [_core_inputs(c, inputs) for c in range(NCORES)]
    res = run_bass_kernel_spmd(nc, in_maps, core_ids=list(range(NCORES)))
    outp = np.zeros((B, S, D), np.float32)
    for c in range(NCORES):
        b, p = c // 2, c % 2
        o = np.asarray(res.results[c]["out"])
        for bi in range(NB):
            g = p + 2 * bi
            outp[b, g * 512:(g + 1) * 512, :] = o[bi * 512:(bi + 1) * 512, :]
    if taps:
        kernel.tap_results = res.results
    return outp
